# version 11
# baseline (speedup 1.0000x reference)
import numpy as np
import ml_dtypes
from contextlib import ExitStack

import concourse.bass as bass
import concourse.mybir as mybir
from concourse.bass_utils import run_bass_kernel_spmd

F32 = mybir.dt.float32
BF16 = mybir.dt.bfloat16
CDT = BF16
AF = mybir.ActivationFunctionType
ALU = mybir.AluOpType
AX = mybir.AxisListType.X

S = 4096
D = 2048
H = 16
C = 128
T = 512
NCH = T // C
NT = D // 128
EPS_LN = 1e-5
EPS_AT = 1e-6
N_CORES = 8


class Sem:
    def __init__(self, name):
        self.name = name
        self.n = 0
        self.h = None


def build_nc(nsc=S // T, debug=False):
    nc = bass.Bass("TRN2", target_bir_lowering=False)

    x_in = nc.dram_tensor("x", [S, D], F32, kind="ExternalInput")
    y_out = nc.dram_tensor("y", [S, D], F32, kind="ExternalOutput")
    w_in = {m: nc.dram_tensor("w_" + m, [D, D], F32, kind="ExternalInput")
            for m in ("q", "k", "v", "o", "f")}
    b_in = {m: nc.dram_tensor("b_" + m, [128, NT], F32, kind="ExternalInput")
            for m in ("q", "k", "v", "o")}
    lnw_in = nc.dram_tensor("lnw", [128, NT], F32, kind="ExternalInput")
    lnb_in = nc.dram_tensor("lnb", [128, NT], F32, kind="ExternalInput")
    identb_in = nc.dram_tensor("identb", [128, 128], BF16, kind="ExternalInput")
    maskb_in = nc.dram_tensor("maskb", [128, 128], BF16, kind="ExternalInput")
    wsc = nc.dram_tensor("wsc", [5, NT, 128, NT, 128], CDT, kind="Internal")
    MIDX = {"q": 0, "k": 1, "v": 2, "o": 3, "f": 4}

    prog = {k: [] for k in ("sync", "gpsimd", "scalar", "vector", "tensor")}
    s_w = [Sem(f"w{i}") for i in range(10)]
    s_xl = [Sem(f"xl{i}") for i in range(2)]
    s_gx = [Sem(f"gx{i}") for i in range(4)]
    s_xr = [Sem(f"xr{i}") for i in range(2)]
    s_sm = {(m, sl): Sem(f"st_{m}{sl}")
            for m in ("q", "k", "v", "o", "f") for sl in range(2)}
    s_ini = Sem("ini")
    s_wfree = Sem("wfree")
    s_pe = Sem("pe")
    s_dv = Sem("dv")
    s_ac = Sem("ac")
    all_sems = (s_w + s_xl + s_gx + s_xr + list(s_sm.values())
                + [s_ini, s_wfree, s_pe, s_dv, s_ac])

    def wait(eng, sem, val):
        if sem is None or val is None or val <= 0:
            return
        prog[eng].append(lambda e, s=sem, v=val: e.wait_ge(s.h, v))

    def op(eng, build_fn, incs=()):
        vals = []
        for s, k in incs:
            s.n += k
            vals.append((s, k))

        def run(e, bf=build_fn, vals=tuple(vals)):
            i = bf(e)
            for s, k in vals:
                i = i.then_inc(s.h, k)
        prog[eng].append(run)

    rel = {}

    def set_rel(key, *deps):
        rel[key] = [d for d in deps if d is not None]

    def add_rel(key, dep):
        rel.setdefault(key, []).append(dep)

    def wait_rel(eng, *keys):
        for key in keys:
            for sem, cnt in rel.get(key, ()):
                wait(eng, sem, cnt)

    def vtiny(build_fn):
        op("vector", build_fn, [(s_dv, 1)])
        wait("vector", s_dv, s_dv.n)

    def atiny(build_fn):
        op("scalar", build_fn, [(s_ac, 1)])
        wait("scalar", s_ac, s_ac.n)

    with ExitStack() as ctx:
        e_ = ctx.enter_context

        NRING = 10
        wring = [e_(nc.sbuf_tensor(f"wring{i}", [128, D // 2], CDT)) for i in range(NRING)]
        xln = [e_(nc.sbuf_tensor(f"xln{i}", [128, D], F32)) for i in range(4)]
        xres = [e_(nc.sbuf_tensor(f"xres{i}", [128, D], F32)) for i in range(2)]
        xnT = e_(nc.sbuf_tensor("xnT", [128, NT * T], CDT))
        xn2T = e_(nc.sbuf_tensor("xn2T", [128, NT * T], CDT))
        qat = e_(nc.sbuf_tensor("qat", [128, NT * T], CDT))
        kfT = e_(nc.sbuf_tensor("kfT", [128, NT * T], CDT))
        vg = e_(nc.sbuf_tensor("vg", [128, NT * T], CDT))
        kvstate = e_(nc.sbuf_tensor("kvstate", [128, D], F32))
        kvbf = e_(nc.sbuf_tensor("kvbf", [128, D], CDT))
        ksumst = e_(nc.sbuf_tensor("ksumst", [128, H], F32))
        ksumbf = e_(nc.sbuf_tensor("ksumbf", [128, H], CDT))
        kscur = e_(nc.sbuf_tensor("kscur", [128, 4 * H], F32))
        kftok = e_(nc.sbuf_tensor("kftok", [128, D], CDT))
        vtok = e_(nc.sbuf_tensor("vtok", [128, D], CDT))
        xns_tm = [e_(nc.sbuf_tensor(f"xns{i}", [128, D], CDT)) for i in range(2)]
        a_m = e_(nc.sbuf_tensor("a_m", [128, D], CDT))
        scr = e_(nc.sbuf_tensor("scr", [128, 4096], F32))
        stats = e_(nc.sbuf_tensor("stats", [128, 16], F32))
        lst = e_(nc.sbuf_tensor("lst", [128, 64], F32))
        biases = {m: e_(nc.sbuf_tensor("bs_" + m, [128, NT], F32))
                  for m in ("q", "k", "v", "o")}
        lnw_sb = e_(nc.sbuf_tensor("lnw_sb", [128, NT], F32))
        lnb_sb = e_(nc.sbuf_tensor("lnb_sb", [128, NT], F32))
        identb = e_(nc.sbuf_tensor("identb_s", [128, 128], BF16))
        maskb = e_(nc.sbuf_tensor("maskb_s", [128, 128], BF16))
        onescol_b = e_(nc.sbuf_tensor("onescol_b", [128, 1], BF16))
        epscol_f = e_(nc.sbuf_tensor("epscol_f", [128, 1], F32))
        onescol_f = e_(nc.sbuf_tensor("onescol_f", [128, 1], F32))
        onesrow_f = e_(nc.sbuf_tensor("onesrow_f", [1, 128], F32))
        alpha1 = e_(nc.sbuf_tensor("alpha1", [128, 1], F32))
        sc_sc = e_(nc.sbuf_tensor("sc_sc", [1, 1], F32))
        rscale = e_(nc.sbuf_tensor("rscale", [128, 1], F32))
        pcast = [e_(nc.sbuf_tensor(f"pcast{i}", [128, D], CDT)) for i in range(2)]
        ternb = a_m

        psA = e_(nc.psum_tensor("psA", [128, D], F32))
        psB = e_(nc.psum_tensor("psB", [128, D], F32))
        psBT = psB.ap().bitcast(BF16)

        for s in all_sems:
            s.h = e_(nc.semaphore(s.name))

        SQ_DUMP = (0, 2048)
        EL_E = [(2048, 2560), (2560, 3072)]
        EL_R = [(3072, 3584), (3584, 4096)]
        RZ = (2048, 4096)
        STMP = [(0, 512), (512, 1024)]

        def psBT_lo3d():
            return psBT[:, 0:2048].rearrange("p (t c) -> p t c", c=128)

        op("gpsimd", lambda e: e.dma_start(out=identb[:, :], in_=identb_in[:, :]),
           [(s_ini, 16)])
        op("gpsimd", lambda e: e.dma_start(out=maskb[:, :], in_=maskb_in[:, :]),
           [(s_ini, 16)])
        for m in ("q", "k", "v", "o"):
            op("gpsimd", lambda e, m=m: e.dma_start(out=biases[m][:, :],
                                                    in_=b_in[m][:, :]),
               [(s_ini, 16)])
        op("gpsimd", lambda e: e.dma_start(out=lnw_sb[:, :], in_=lnw_in[:, :]),
           [(s_ini, 16)])
        op("gpsimd", lambda e: e.dma_start(out=lnb_sb[:, :], in_=lnb_in[:, :]),
           [(s_ini, 16)])
        init_gdma = s_ini.n

        wait("scalar", s_ini, init_gdma)
        wait("vector", s_ini, init_gdma)
        wait("tensor", s_ini, init_gdma)

        op("vector", lambda e: e.memset(onescol_b[:, :], 1.0))
        op("vector", lambda e: e.memset(epscol_f[:, :], EPS_AT))
        op("vector", lambda e: e.memset(onescol_f[:, :], 1.0))
        op("vector", lambda e: e.memset(onesrow_f[:, :], 1.0))
        op("vector", lambda e: e.memset(alpha1[:, :], 1.0))
        op("vector", lambda e: e.memset(kvstate[:, :], 0.0))
        op("vector", lambda e: e.memset(kvbf[:, :], 0.0))
        op("vector", lambda e: e.memset(ksumst[:, :], 0.0))
        op("vector", lambda e: e.memset(ksumbf[:, :], 0.0), [(s_dv, 1)])
        init_dv = s_dv.n

        psA_bank_rel = [None] * 4
        pro_steps = []
        cast_rel = [None, None]
        loadslot_rel = [None, None]

        def mk_step_convert(m, r, mode):
            sl = r % 2

            def step():
                lr = loadslot_rel[sl]
                if lr is not None:
                    wait("sync", lr[0], lr[1])
                op("sync", lambda e, m=m, r=r, sl=sl: e.dma_start(
                    out=xres[sl][:, :], in_=w_in[m][r * 128:(r + 1) * 128, :]),
                   [(s_xl[sl], 16)])
                w_loaded = s_xl[sl].n
                wait("vector", s_xl[sl], w_loaded)
                if mode == "absred":
                    op("vector", lambda e, r=r, sl=sl: e.reduce_sum(
                        stats[:, r:r + 1], xres[sl][:, :], axis=AX,
                        apply_absolute_value=True), [(s_dv, 1)])
                    loadslot_rel[sl] = (s_dv, s_dv.n)
                    return
                cslot = pcast[sl]
                cr = cast_rel[sl]
                if cr is not None:
                    wait("vector", cr[0], cr[1])
                if mode == "cast":
                    op("vector", lambda e, sl=sl, cs=cslot: e.tensor_copy(
                        cs[:, :], xres[sl][:, :]), [(s_dv, 1)])
                else:
                    wait("vector", s_dv, s_dv.n)
                    op("vector", lambda e, sl=sl, cs=cslot: e.tensor_scalar(
                        cs[:, :], xres[sl][:, :], rscale[:, :], 0.5,
                        ALU.mult, ALU.is_gt))
                    op("vector", lambda e, sl=sl: e.tensor_scalar(
                        ternb[:, :], xres[sl][:, :], rscale[:, :], -0.5,
                        ALU.mult, ALU.is_lt), [(s_dv, 1)])
                    wait("vector", s_dv, s_dv.n)
                    op("vector", lambda e, cs=cslot: e.tensor_tensor(
                        cs[:, :], cs[:, :], ternb[:, :], ALU.subtract),
                       [(s_dv, 1)])
                    set_rel("a_m", (s_dv, s_dv.n))
                loadslot_rel[sl] = (s_dv, s_dv.n)
                wait("gpsimd", s_dv, s_dv.n)
                for half in range(2):
                    def bf(e, slot=cslot, m=m, r=r, half=half):
                        s2 = slot[:, :].rearrange("p (t d) -> p t d", d=128)
                        s2 = s2[:, half * 8:(half + 1) * 8, :]
                        dst = wsc[MIDX[m], half * 8:(half + 1) * 8, :, r, :]
                        return e.dma_start(out=dst.transpose([1, 0, 2]), in_=s2)
                    op("gpsimd", bf, [(s_sm[(m, sl)], 16)])
                cast_rel[sl] = (s_sm[(m, sl)], s_sm[(m, sl)].n)
            return step

        def mk_step_scale():
            def step():
                wait("vector", s_dv, s_dv.n)
                op("vector", lambda e: e.reduce_sum(
                    scr[:, 0:1], stats[:, 0:NT], axis=AX), [(s_dv, 1)])
                wait("vector", s_dv, s_dv.n)
                wait("tensor", s_dv, s_dv.n)
                r0 = psA_bank_rel[0]
                if r0 is not None:
                    wait("tensor", r0[0], r0[1])
                op("tensor", lambda e: e.matmul(
                    psA[0:1, 0:1], scr[:, 0:1], onescol_f[:, :],
                    start=True, stop=True), [(s_pe, 1)])
                wait("vector", s_pe, s_pe.n)
                vtiny(lambda e: e.tensor_scalar(
                    sc_sc[:, :], psA[0:1, 0:1], 1.0 / (D * D), None, ALU.mult))
                wait("tensor", s_dv, s_dv.n)
                op("tensor", lambda e: e.matmul(
                    psA[:, 1:2], onesrow_f[:, :], sc_sc[:, :],
                    start=True, stop=True), [(s_pe, 1)])
                wait("vector", s_pe, s_pe.n)
                vtiny(lambda e: e.tensor_copy(scr[:, 1:2], psA[:, 1:2]))
                vtiny(lambda e: e.reciprocal(rscale[:, :], scr[:, 1:2]))
                psA_bank_rel[0] = (s_dv, s_dv.n)
            return step

        for m in ("q", "k", "v", "o"):
            for r in range(NT):
                pro_steps.append(mk_step_convert(m, r, "cast"))
        for r in range(NT):
            pro_steps.append(mk_step_convert("f", r, "absred"))
        pro_steps.append(mk_step_scale())
        for r in range(NT):
            pro_steps.append(mk_step_convert("f", r, "tern"))
        PRO_READY = {"q": 16, "k": 32, "v": 48, "o": 64, "f": len(pro_steps)}
        pumped = [0]

        def pump(n):
            while n > 0 and pumped[0] < len(pro_steps):
                pro_steps[pumped[0]]()
                pumped[0] += 1
                n -= 1

        def pump_until(idx):
            pump(max(0, idx - pumped[0]))

        wtile_ctr = [0]
        pending_wfree = [0]

        def stream_whalf(m, to, half):
            k = wtile_ctr[0]
            wtile_ctr[0] += 1
            if k >= NRING:
                wait("sync", s_wfree, k - (NRING - 1))
            if to == 0 and half == 0:
                pump_until(PRO_READY[m])
                wait("sync", s_sm[(m, 0)], s_sm[(m, 0)].n)
                wait("sync", s_sm[(m, 1)], s_sm[(m, 1)].n)
            op("sync", lambda e, m=m, to=to, k=k, half=half: e.dma_start(
                out=wring[k % NRING][:, :].rearrange("p (t d) -> p t d", d=128),
                in_=wsc[MIDX[m], to, :, half * 8:(half + 1) * 8, :]),
               [(s_w[k % NRING], 16)])
            return k, (s_w[k % NRING], s_w[k % NRING].n)

        def load_xln(c):
            sl = c % 4
            wait_rel("gpsimd", f"xln{sl}")
            op("gpsimd", lambda e, c=c, sl=sl: e.dma_start(
                out=xln[sl][:, :], in_=x_in[c * 128:(c + 1) * 128, :]),
               [(s_gx[sl], 16)])
            return (s_gx[sl], s_gx[sl].n)

        def load_xres(c):
            sl = c % 2
            wait_rel("gpsimd", f"xres{sl}")
            op("gpsimd", lambda e, c=c, sl=sl: e.dma_start(
                out=xres[sl][:, :], in_=x_in[c * 128:(c + 1) * 128, :]),
               [(s_xr[sl], 16)])
            return (s_xr[sl], s_xr[sl].n)

        ac_kvbf = 0
        dv_kvbf_state = [0]
        xres_handover = [False]

        pend_ln = {}

        def emit_ln_A(sc2, gate_dv):
            L_SUM, L_SQ, L_MEAN, L_M2, L_VAR, L_U, L_RU, L_W2 = 0, 4, 8, 12, 16, 20, 24, 28
            L_LNU, L_R1, L_LNW2, L_R2, L_NMR, L_SC2, L_NMR2 = 32, 36, 40, 44, 48, 52, 56
            loaded = {}
            for j in range(NCH):
                loaded[j] = load_xln(sc2 * NCH + j)
            ac_sq = 0
            for j in range(NCH):
                xs = xln[(sc2 * NCH + j) % 4]
                wait("scalar", loaded[j][0], loaded[j][1])
                if j == 0 and gate_dv:
                    wait("scalar", s_dv, gate_dv)
                if j > 0:
                    wait("scalar", s_ac, ac_sq)
                op("scalar", lambda e, xs=xs, j=j: e.activation(
                    scr[:, RZ[0]:RZ[1]], xs[:, :], AF.Square,
                    accum_out=lst[:, L_SQ + j:L_SQ + j + 1]), [(s_ac, 1)])
                ac_sq = s_ac.n
                wait("vector", loaded[j][0], loaded[j][1])
                op("vector", lambda e, xs=xs, j=j: e.reduce_sum(
                    lst[:, L_SUM + j:L_SUM + j + 1], xs[:, :], axis=AX),
                   [(s_dv, 1)])
            wait("vector", s_dv, s_dv.n)
            c4 = lambda base: lst[:, base:base + 4]
            vtiny(lambda e: e.tensor_scalar(
                c4(L_MEAN), c4(L_SUM), 1.0 / D, None, ALU.mult))
            vtiny(lambda e: e.tensor_tensor(
                c4(L_M2), c4(L_MEAN), c4(L_MEAN), ALU.mult))
            wait("vector", s_ac, ac_sq)
            vtiny(lambda e: e.scalar_tensor_tensor(
                c4(L_VAR), c4(L_SQ), 1.0 / D, c4(L_M2), ALU.mult, ALU.subtract))
            vtiny(lambda e: e.tensor_scalar(
                c4(L_U), c4(L_VAR), EPS_LN, None, ALU.add))
            vtiny(lambda e: e.reciprocal(c4(L_RU), c4(L_U)))
            vtiny(lambda e: e.tensor_tensor(
                c4(L_W2), c4(L_VAR), c4(L_RU), ALU.mult))
            vtiny(lambda e: e.tensor_scalar(
                c4(L_W2), c4(L_W2), EPS_LN, None, ALU.add))
            wait("scalar", s_dv, s_dv.n)
            atiny(lambda e: e.activation(c4(L_LNU), c4(L_U), AF.Ln))
            atiny(lambda e: e.activation(c4(L_R1), c4(L_LNU), AF.Exp, scale=-0.5))
            atiny(lambda e: e.activation(c4(L_LNW2), c4(L_W2), AF.Ln))
            atiny(lambda e: e.activation(c4(L_R2), c4(L_LNW2), AF.Exp, scale=-0.5))
            wait("vector", s_ac, s_ac.n)
            vtiny(lambda e: e.scalar_tensor_tensor(
                c4(L_NMR), c4(L_MEAN), -1.0, c4(L_R1), ALU.mult, ALU.mult))
            vtiny(lambda e: e.tensor_tensor(
                c4(L_SC2), c4(L_R1), c4(L_R2), ALU.mult))
            vtiny(lambda e: e.scalar_tensor_tensor(
                c4(L_NMR2), c4(L_MEAN), -1.0, c4(L_SC2), ALU.mult, ALU.mult))
            dv_aff = s_dv.n
            wait("scalar", s_dv, dv_aff)
            ac_applies = {}
            for j in range(2):
                c = sc2 * NCH + j
                xs = xln[c % 4]
                xntm = [kftok, vtok][j % 2]
                xnstm = xns_tm[j % 2]
                wait_rel("scalar", "kftok" if j % 2 == 0 else "vtok")
                op("scalar", lambda e, xs=xs, xm=xntm, j=j: e.activation(
                    xm[:, :], xs[:, :], AF.Identity,
                    bias=lst[:, L_NMR + j:L_NMR + j + 1],
                    scale=lst[:, L_R1 + j:L_R1 + j + 1]), [(s_ac, 1)])
                wait_rel("scalar", f"xns{j % 2}")
                op("scalar", lambda e, xs=xs, xm=xnstm, j=j: e.activation(
                    xm[:, :], xs[:, :], AF.Identity,
                    bias=lst[:, L_NMR2 + j:L_NMR2 + j + 1],
                    scale=lst[:, L_SC2 + j:L_SC2 + j + 1]), [(s_ac, 1)])
                ac_applies[j] = s_ac.n
                set_rel(f"xln{c % 4}", (s_ac, s_ac.n))
            pend_ln[sc2] = (ac_applies, s_dv.n)

        def emit_apply(sc2, j, ac_applies):
            L_R1, L_NMR, L_SC2, L_NMR2 = 36, 48, 52, 56
            c = sc2 * NCH + j
            xs = xln[c % 4]
            xntm = [kftok, vtok][j % 2]
            xnstm = xns_tm[j % 2]
            wait_rel("scalar", "kftok" if j % 2 == 0 else "vtok")
            op("scalar", lambda e, xs=xs, xm=xntm, j=j: e.activation(
                xm[:, :], xs[:, :], AF.Identity,
                bias=lst[:, L_NMR + j:L_NMR + j + 1],
                scale=lst[:, L_R1 + j:L_R1 + j + 1]), [(s_ac, 1)])
            wait_rel("scalar", f"xns{j % 2}")
            op("scalar", lambda e, xs=xs, xm=xnstm, j=j: e.activation(
                xm[:, :], xs[:, :], AF.Identity,
                bias=lst[:, L_NMR2 + j:L_NMR2 + j + 1],
                scale=lst[:, L_SC2 + j:L_SC2 + j + 1]), [(s_ac, 1)])
            ac_applies[j] = s_ac.n
            set_rel(f"xln{c % 4}", (s_ac, s_ac.n))

        def emit_ln_B(sc2):
            ac_applies, dv_aff = pend_ln.pop(sc2)
            for j in range(NCH):
                xntm = [kftok, vtok][j % 2]
                xnstm = xns_tm[j % 2]
                wait("tensor", s_ac, ac_applies[j])
                wait("tensor", s_ini, init_gdma)
                wait_rel("tensor", "psB")
                for t in range(NT):
                    incs = [(s_pe, 1)] if t == NT - 1 else []
                    op("tensor", lambda e, t=t, xm=xnstm: e.transpose(
                        psBT[:, 2048 + t * 128:2048 + (t + 1) * 128],
                        xm[:, t * 128:(t + 1) * 128], identb[:, :]), incs)
                pe_tr2 = s_pe.n
                for t in range(NT):
                    incs = [(s_pe, 1)] if t == NT - 1 else []
                    op("tensor", lambda e, t=t, xm=xntm: e.transpose(
                        psBT[:, t * 128:(t + 1) * 128],
                        xm[:, t * 128:(t + 1) * 128], identb[:, :]), incs)
                pe_tr1 = s_pe.n
                set_rel("kftok" if j % 2 == 0 else "vtok", (s_pe, pe_tr1))
                set_rel(f"xns{j % 2}", (s_pe, pe_tr2))
                if j + 2 < NCH:
                    emit_apply(sc2, j + 2, ac_applies)
                wait("vector", s_pe, pe_tr1)
                wait_rel("vector", "xnT")
                op("vector", lambda e, j=j: e.tensor_copy(
                    bass.AP(xnT, j * 128, [[NT * T, 128], [T, NT], [1, 128]]),
                    psBT_lo3d()), [(s_dv, 1)])
                dv_xnT = s_dv.n
                wait("vector", s_pe, pe_tr2)
                wait_rel("vector", "xn2T")
                for t in range(NT):
                    incs = [(s_dv, 1)] if t == NT - 1 else []
                    op("vector", lambda e, t=t, j=j: e.tensor_scalar(
                        xn2T[:, t * T + j * 128: t * T + (j + 1) * 128],
                        psBT[:, 2048 + t * 128:2048 + (t + 1) * 128],
                        lnw_sb[:, t:t + 1], lnb_sb[:, t:t + 1],
                        ALU.mult, ALU.add), incs)
                set_rel("psB", (s_dv, s_dv.n))
                pump(2)
            return s_dv.n, s_ac.n

        for sc in range(nsc):
            if sc == 0:
                emit_ln_A(0, 0)
            dv_ln_done, ac_ln_done = emit_ln_B(sc)

            def proj_tile(m, to, evac_fn, rhs_buf, rhs_deps):
                kA, wnA = stream_whalf(m, to, 0)
                kB, wnB = stream_whalf(m, to, 1)
                bk = to % 4
                r0 = psA_bank_rel[bk]
                if r0 is not None:
                    wait("tensor", r0[0], r0[1])
                if to == 0:
                    for sem, cnt in rhs_deps:
                        wait("tensor", sem, cnt)
                wait("tensor", wnA[0], wnA[1])
                for ti in range(NT):
                    k = kA if ti < 8 else kB
                    incs = []
                    if ti == 0 and pending_wfree[0]:
                        incs = [(s_wfree, pending_wfree[0])]
                        pending_wfree[0] = 0
                    if ti == 8:
                        incs = [(s_wfree, 1)]
                        wait("tensor", wnB[0], wnB[1])
                    if ti == NT - 1:
                        incs = [(s_pe, 1)]
                    op("tensor", lambda e, k=k, ti=ti, bk=bk, rb=rhs_buf: e.matmul(
                        psA[:, bk * 512:(bk + 1) * 512],
                        wring[k % NRING][:, (ti % 8) * 128:(ti % 8 + 1) * 128],
                        rb[:, ti * T:(ti + 1) * T],
                        start=(ti == 0), stop=(ti == NT - 1)), incs)
                pending_wfree[0] += 1
                evac_fn(to, bk, s_pe.n)
                pump(2)

            def proj_phase(m, evac_fn, rhs_buf, rhs_deps):
                for to in range(NT):
                    proj_tile(m, to, evac_fn, rhs_buf, rhs_deps)

            def mk_evac_qk(dst, bias_key):
                def evac(to, bk, pe_n):
                    esl = EL_E[to % 2]
                    rsl = EL_R[to % 2]
                    wait("scalar", s_pe, pe_n)
                    wait_rel("scalar", f"el{to % 2}")
                    op("scalar", lambda e, bk=bk, to=to, esl=esl: e.activation(
                        scr[:, esl[0]:esl[1]], psA[:, bk * 512:(bk + 1) * 512],
                        AF.Exp, bias=biases[bias_key][:, to:to + 1]), [(s_ac, 1)])
                    op("scalar", lambda e, bk=bk, to=to, rsl=rsl: e.activation(
                        scr[:, rsl[0]:rsl[1]], psA[:, bk * 512:(bk + 1) * 512],
                        AF.Relu, bias=biases[bias_key][:, to:to + 1]), [(s_ac, 1)])
                    psA_bank_rel[bk] = (s_ac, s_ac.n)
                    wait("vector", s_ac, s_ac.n)
                    op("vector", lambda e, to=to, esl=esl, rsl=rsl:
                       e.scalar_tensor_tensor(
                           dst[:, to * T:(to + 1) * T], scr[:, esl[0]:esl[1]], 1.0,
                           scr[:, rsl[0]:rsl[1]], ALU.min, ALU.add), [(s_dv, 1)])
                    set_rel(f"el{to % 2}", (s_dv, s_dv.n))
                return evac

            proj_phase("q", mk_evac_qk(qat, "q"), xnT, [(s_dv, dv_ln_done)])
            q_done_dv = s_dv.n
            proj_phase("k", mk_evac_qk(kfT, "k"), xnT, [(s_dv, dv_ln_done)])
            k_done_dv = s_dv.n
            wait("vector", s_dv, s_dv.n)
            for j in range(NCH):
                op("vector", lambda e, j=j: e.reduce_sum(
                    bass.AP(kscur, j * H, [[4 * H, 128], [1, H], [0, 1]]),
                    bass.AP(kfT, j * 128, [[NT * T, 128], [T, H], [1, 128]]),
                    axis=AX), [(s_dv, 1)])
            dv_ksum4 = s_dv.n

            def evac_v(to, bk, pe_n):
                wait("scalar", s_pe, pe_n)
                op("scalar", lambda e, to=to, bk=bk: e.activation(
                    vg[:, to * T:(to + 1) * T], psA[:, bk * 512:(bk + 1) * 512],
                    AF.Identity, bias=biases["v"][:, to:to + 1],
                    scale=1.0), [(s_ac, 1)])
                psA_bank_rel[bk] = (s_ac, s_ac.n)

            proj_phase("v", evac_v, xnT, [(s_dv, dv_ln_done)])
            v_done_ac = s_ac.n

            def evac_f(to, bk, pe_n):
                wait("scalar", s_pe, pe_n)
                op("scalar", lambda e, to=to, bk=bk: e.activation(
                    xnT[:, to * T:(to + 1) * T], psA[:, bk * 512:(bk + 1) * 512],
                    AF.Silu), [(s_ac, 1)])
                psA_bank_rel[bk] = (s_ac, s_ac.n)

            for j in range(NCH):
                wait_rel("tensor", "psB")
                wait("tensor", s_dv, k_done_dv)
                wait("tensor", s_ac, v_done_ac)
                for h in range(H):
                    incs = [(s_pe, 1)] if h == H - 1 else []
                    op("tensor", lambda e, h=h, j=j: e.transpose(
                        psBT[:, h * 128:(h + 1) * 128],
                        kfT[:, h * T + j * 128: h * T + (j + 1) * 128],
                        identb[:, :]), incs)
                pe_kft = s_pe.n
                for h in range(H):
                    incs = [(s_pe, 1)] if h == H - 1 else []
                    op("tensor", lambda e, h=h, j=j: e.transpose(
                        psBT[:, 2048 + h * 128:2048 + (h + 1) * 128],
                        vg[:, h * T + j * 128: h * T + (j + 1) * 128],
                        identb[:, :]), incs)
                pe_vt = s_pe.n
                wait("scalar", s_pe, pe_kft)
                wait_rel("scalar", "kftok")
                op("scalar", lambda e: e.copy(
                    kftok[:, :], psBT[:, 0:2048]), [(s_ac, 1)])
                ac_kftok = s_ac.n
                wait("scalar", s_pe, pe_vt)
                wait_rel("scalar", "vtok")
                op("scalar", lambda e: e.copy(
                    vtok[:, :], psBT[:, 2048:4096]), [(s_ac, 1)])
                ac_vtok = s_ac.n
                wait("tensor", s_dv, q_done_dv)
                for bk in range(4):
                    r0 = psA_bank_rel[bk]
                    if r0 is not None:
                        wait("tensor", r0[0], r0[1])
                for h in range(H):
                    incs = [(s_pe, 1)] if h == H - 1 else []
                    op("tensor", lambda e, h=h, j=j: e.matmul(
                        psA[:, h * 128:(h + 1) * 128],
                        kfT[:, h * T + j * 128: h * T + (j + 1) * 128],
                        qat[:, h * T + j * 128: h * T + (j + 1) * 128],
                        start=True, stop=True), incs)
                pe_a = s_pe.n
                wait("vector", s_pe, pe_a)
                wait_rel("vector", "a_m")
                op("vector", lambda e: e.tensor_tensor(
                    a_m[:, 0:1024].rearrange("p (t c) -> p t c", c=128),
                    psA[:, 0:1024].rearrange("p (t c) -> p t c", c=128),
                    maskb[:, :].unsqueeze(1).broadcast_to([128, H // 2, 128]),
                    ALU.mult), [(s_dv, 1)])
                dv_am_a = s_dv.n
                op("vector", lambda e: e.tensor_tensor(
                    a_m[:, 1024:2048].rearrange("p (t c) -> p t c", c=128),
                    psA[:, 1024:2048].rearrange("p (t c) -> p t c", c=128),
                    maskb[:, :].unsqueeze(1).broadcast_to([128, H // 2, 128]),
                    ALU.mult), [(s_dv, 1)])
                dv_am_b = s_dv.n
                wait("tensor", s_ac, ac_vtok)
                wait("tensor", s_ac, ac_kftok)
                wait("tensor", s_dv, dv_kvbf_state[0])
                for h in range(H):
                    if h == 0:
                        wait("tensor", s_dv, dv_am_a)
                    if h == H // 2:
                        wait("tensor", s_dv, dv_am_b)
                    op("tensor", lambda e, h=h, j=j: e.matmul(
                        psB[:, h * 128:(h + 1) * 128],
                        kvbf[:, h * 128:(h + 1) * 128],
                        qat[:, h * T + j * 128: h * T + (j + 1) * 128],
                        start=True, stop=False))
                    incs = [(s_pe, 1)] if h == H - 1 else []
                    op("tensor", lambda e, h=h: e.matmul(
                        psB[:, h * 128:(h + 1) * 128],
                        vtok[:, h * 128:(h + 1) * 128],
                        a_m[:, h * 128:(h + 1) * 128],
                        start=False, stop=True), incs)
                pe_io = s_pe.n
                for h in range(H):
                    op("tensor", lambda e, h=h, j=j: e.matmul(
                        psA[:, h * 128:(h + 1) * 128],
                        bass.AP(ksumbf, h, [[H, 128], [0, 128]]),
                        qat[:, h * T + j * 128: h * T + (j + 1) * 128],
                        start=True, stop=False))
                    incs = [(s_pe, 1)] if h == H - 1 else []
                    op("tensor", lambda e, h=h: e.matmul(
                        psA[:, h * 128:(h + 1) * 128],
                        bass.AP(onescol_b, 0, [[1, 128], [0, 128]]),
                        a_m[:, h * 128:(h + 1) * 128],
                        start=False, stop=True), incs)
                pe_z = s_pe.n
                wait("scalar", s_pe, pe_z)
                op("scalar", lambda e: e.activation(
                    scr[:, SQ_DUMP[0]:SQ_DUMP[1]], psA[:, :], AF.Ln,
                    bias=epscol_f[:, :]), [(s_ac, 1)])
                ac_lnz = s_ac.n
                wait("scalar", s_ac, ac_lnz)
                op("scalar", lambda e: e.activation(
                    scr[:, RZ[0]:RZ[1]], scr[:, SQ_DUMP[0]:SQ_DUMP[1]],
                    AF.Exp, scale=-1.0), [(s_ac, 1)])
                ac_rz = s_ac.n
                wait("tensor", s_ac, ac_lnz)
                for h in range(H):
                    incs = [(s_pe, 1)] if h == H - 1 else []
                    op("tensor", lambda e, h=h: e.matmul(
                        psA[:, h * 128:(h + 1) * 128],
                        kftok[:, h * 128:(h + 1) * 128],
                        vtok[:, h * 128:(h + 1) * 128],
                        start=True, stop=True), incs)
                pe_kv = s_pe.n
                wait("vector", s_pe, pe_io)
                wait("vector", s_ac, ac_rz)
                op("vector", lambda e, j=j: e.tensor_tensor(
                    bass.AP(qat, j * 128, [[NT * T, 128], [T, H], [1, 128]]),
                    psB[:, :].rearrange("p (t c) -> p t c", c=128),
                    scr[:, RZ[0]:RZ[1]].rearrange("p (t c) -> p t c", c=128),
                    ALU.mult), [(s_dv, 1)])
                set_rel("psB", (s_dv, s_dv.n), (s_ac, ac_vtok))
                wait("vector", s_pe, pe_kv)
                op("vector", lambda e: e.tensor_tensor(
                    kvbf[:, :], kvstate[:, :], psA[:, :], ALU.add), [(s_dv, 1)])
                dv_kvbf = s_dv.n
                wait("vector", s_dv, dv_ksum4)
                op("vector", lambda e, j=j: e.tensor_tensor(
                    ksumbf[:, :], ksumst[:, :],
                    kscur[:, j * H:(j + 1) * H], ALU.add), [(s_dv, 1)])
                dv_ksbf = s_dv.n
                op("vector", lambda e: e.tensor_tensor(
                    kvstate[:, :], kvstate[:, :], psA[:, :], ALU.add), [(s_dv, 1)])
                dv_kvupd = s_dv.n
                for bk in range(4):
                    psA_bank_rel[bk] = (s_dv, dv_kvupd)
                wait("vector", s_dv, dv_ksbf)
                vtiny(lambda e, j=j: e.tensor_tensor(
                    ksumst[:, :], ksumst[:, :],
                    kscur[:, j * H:(j + 1) * H], ALU.add))
                dv_kvbf_state[0] = dv_kvbf
                set_rel("kftok", (s_pe, pe_kv))
                set_rel("vtok", (s_pe, pe_kv))
                set_rel("a_m", (s_pe, pe_io))
                for to in range(4 * j, 4 * j + 4):
                    proj_tile("f", to, evac_f, xn2T,
                              [(s_ac, ac_ln_done)])
            f_done_ac = s_ac.n
            attn_dv_done = s_dv.n
            set_rel("el0", (s_dv, attn_dv_done))
            set_rel("el1", (s_dv, attn_dv_done))

            if sc + 1 < nsc:
                emit_ln_A(sc + 1, attn_dv_done)

            if not xres_handover[0]:
                pump_until(len(pro_steps))
                for sl2 in range(2):
                    if loadslot_rel[sl2] is not None:
                        set_rel(f"xres{sl2}", loadslot_rel[sl2])
                xres_handover[0] = True
            xres_loaded = {}
            xres_loaded[0] = load_xres(sc * NCH + 0)
            xres_loaded[1] = load_xres(sc * NCH + 1)

            dv_sg = {}

            def evac_o(to, bk, pe_n):
                ssl = STMP[to % 2]
                wait("scalar", s_pe, pe_n)
                wait_rel("scalar", f"st{to % 2}")
                op("scalar", lambda e, to=to, bk=bk, ssl=ssl: e.activation(
                    scr[:, ssl[0]:ssl[1]], psA[:, bk * 512:(bk + 1) * 512],
                    AF.Identity, bias=biases["o"][:, to:to + 1], scale=1.0,
                    ), [(s_ac, 1)])
                psA_bank_rel[bk] = (s_ac, s_ac.n)
                wait("vector", s_ac, s_ac.n)
                if to == 0:
                    wait("vector", s_ac, f_done_ac)
                op("vector", lambda e, to=to, ssl=ssl: e.tensor_tensor(
                    vg[:, to * T:(to + 1) * T], scr[:, ssl[0]:ssl[1]],
                    xnT[:, to * T:(to + 1) * T], ALU.mult), [(s_dv, 1)])
                dv_sg[to] = s_dv.n
                set_rel(f"st{to % 2}", (s_dv, s_dv.n))

            proj_phase("o", evac_o, qat, [(s_dv, attn_dv_done)])
            sg_dv_done = s_dv.n

            for j in range(NCH):
                c = sc * NCH + j
                wait_rel("tensor", "psB")
                for t in range(NT):
                    wait("tensor", s_dv, dv_sg[t])
                    incs = [(s_pe, 1)] if t == NT - 1 else []
                    op("tensor", lambda e, t=t, j=j: e.transpose(
                        psBT[:, t * 128:(t + 1) * 128],
                        vg[:, t * T + j * 128: t * T + (j + 1) * 128],
                        identb[:, :]), incs)
                pe_sg = s_pe.n
                sl = c % 2
                wait("vector", s_pe, pe_sg)
                wait("vector", xres_loaded[j][0], xres_loaded[j][1])
                op("vector", lambda e, sl=sl: e.tensor_tensor(
                    xres[sl][:, :].rearrange("p (t d) -> p t d", d=128),
                    psBT_lo3d(),
                    xres[sl][:, :].rearrange("p (t d) -> p t d", d=128),
                    ALU.add), [(s_dv, 1)])
                dv_out = s_dv.n
                set_rel("psB", (s_dv, dv_out))
                wait("gpsimd", s_dv, dv_out)
                op("gpsimd", lambda e, c=c, sl=sl: e.dma_start(
                    out=y_out[c * 128:(c + 1) * 128, :], in_=xres[sl][:, :]),
                   [(s_xr[sl], 16)])
                set_rel(f"xres{sl}", (s_xr[sl], s_xr[sl].n))
                if j + 2 < NCH:
                    xres_loaded[j + 2] = load_xres(sc * NCH + j + 2)
            set_rel("xnT", (s_dv, sg_dv_done))
            set_rel("vg_next", (s_pe, s_pe.n))

        wait("gpsimd", s_xr[0], s_xr[0].n)
        wait("gpsimd", s_xr[1], s_xr[1].n)

        if debug:
            dbg_specs = [
                ("xnT", xnT, CDT), ("xn2T", xn2T, CDT), ("qat", qat, CDT),
                ("kfT", kfT, CDT), ("vg", vg, CDT), ("kvstate", kvstate, F32),
                ("stats", stats, F32), ("rscale", rscale, F32),
                ("kftok", kftok, CDT), ("vtok", vtok, CDT), ("a_m", a_m, CDT),
                ("ksumst", ksumst, F32), ("scr", scr, F32),
                ("kvbf", kvbf, CDT),
            ]
            wait("gpsimd", s_dv, s_dv.n)
            wait("gpsimd", s_ac, s_ac.n)
            wait("gpsimd", s_pe, s_pe.n)
            for name, buf, dt in dbg_specs:
                dbg_out = nc.dram_tensor("dbg_" + name, list(buf.shape), dt,
                                         kind="ExternalOutput")
                op("gpsimd", lambda e, o=dbg_out, b=buf: e.dma_start(
                    out=o[:, :], in_=b[:, :]), [(s_ini, 16)])
            wait("gpsimd", s_ini, s_ini.n)

        block = e_(nc.Block())

        @block.sync
        def _(sync):
            for fn in prog["sync"]:
                fn(sync)

        @block.gpsimd
        def _(gpsimd):
            for fn in prog["gpsimd"]:
                fn(gpsimd)

        @block.scalar
        def _(scalar):
            for fn in prog["scalar"]:
                fn(scalar)

        @block.vector
        def _(vector):
            for fn in prog["vector"]:
                fn(vector)

        @block.tensor
        def _(tensor):
            for fn in prog["tensor"]:
                fn(tensor)

    return nc


_NC_CACHE = {}


def _get_nc(nsc):
    if nsc not in _NC_CACHE:
        _NC_CACHE[nsc] = build_nc(nsc)
    return _NC_CACHE[nsc]


def kernel(L0_attn_x, L0_attn_Wq, L0_attn_bq, L0_attn_Wk, L0_attn_bk,
           L0_attn_Wv, L0_attn_bv, L0_attn_Wo, L0_attn_bo,
           L0_ffn_W, L0_ffn_ln_w, L0_ffn_ln_b):
    nc = _get_nc(S // T)
    x = np.ascontiguousarray(np.asarray(L0_attn_x, dtype=np.float32))

    def colmajor16(v):
        return np.ascontiguousarray(
            np.asarray(v, dtype=np.float32).reshape(NT, 128).T)

    common = {
        "w_q": np.ascontiguousarray(np.asarray(L0_attn_Wq, dtype=np.float32)),
        "w_k": np.ascontiguousarray(np.asarray(L0_attn_Wk, dtype=np.float32)),
        "w_v": np.ascontiguousarray(np.asarray(L0_attn_Wv, dtype=np.float32)),
        "w_o": np.ascontiguousarray(np.asarray(L0_attn_Wo, dtype=np.float32)),
        "w_f": np.ascontiguousarray(np.asarray(L0_ffn_W, dtype=np.float32)),
        "b_q": colmajor16(L0_attn_bq),
        "b_k": colmajor16(L0_attn_bk),
        "b_v": colmajor16(L0_attn_bv),
        "b_o": colmajor16(L0_attn_bo),
        "lnw": colmajor16(L0_ffn_ln_w),
        "lnb": colmajor16(L0_ffn_ln_b),
        "identb": np.eye(128, dtype=ml_dtypes.bfloat16),
        "maskb": np.triu(np.ones((128, 128), dtype=ml_dtypes.bfloat16)),
    }
    in_maps = []
    for b in range(N_CORES):
        m = dict(common)
        m["x"] = np.ascontiguousarray(x[b])
        in_maps.append(m)
    res = run_bass_kernel_spmd(nc, in_maps, core_ids=list(range(N_CORES)))
    out = np.stack([res.results[b]["y"] for b in range(N_CORES)], axis=0)
    return out


# version 12
# speedup vs baseline: 1.0293x; 1.0293x over previous
import numpy as np
import ml_dtypes
from contextlib import ExitStack

import concourse.bass as bass
import concourse.mybir as mybir
from concourse.bass_utils import run_bass_kernel_spmd

F32 = mybir.dt.float32
BF16 = mybir.dt.bfloat16
CDT = BF16
AF = mybir.ActivationFunctionType
ALU = mybir.AluOpType
AX = mybir.AxisListType.X

S = 4096
D = 2048
H = 16
C = 128
T = 512
NCH = T // C
NT = D // 128
EPS_LN = 1e-5
EPS_AT = 1e-6
N_CORES = 8


class Sem:
    def __init__(self, name):
        self.name = name
        self.n = 0
        self.h = None


def build_nc(nsc=S // T, debug=False):
    nc = bass.Bass("TRN2", target_bir_lowering=False)

    x_in = nc.dram_tensor("x", [S, D], F32, kind="ExternalInput")
    y_out = nc.dram_tensor("y", [S, D], F32, kind="ExternalOutput")
    w_in = {m: nc.dram_tensor("w_" + m, [D, D], F32, kind="ExternalInput")
            for m in ("q", "k", "v", "o", "f")}
    b_in = {m: nc.dram_tensor("b_" + m, [128, NT], F32, kind="ExternalInput")
            for m in ("q", "k", "v", "o")}
    lnw_in = nc.dram_tensor("lnw", [128, NT], F32, kind="ExternalInput")
    lnb_in = nc.dram_tensor("lnb", [128, NT], F32, kind="ExternalInput")
    identb_in = nc.dram_tensor("identb", [128, 128], BF16, kind="ExternalInput")
    maskb_in = nc.dram_tensor("maskb", [128, 128], BF16, kind="ExternalInput")
    wsc = nc.dram_tensor("wsc", [5, NT, 128, NT, 128], CDT, kind="Internal")
    MIDX = {"q": 0, "k": 1, "v": 2, "o": 3, "f": 4}

    prog = {k: [] for k in ("sync", "gpsimd", "scalar", "vector", "tensor")}
    s_w = [Sem(f"w{i}") for i in range(10)]
    s_xl = [Sem(f"xl{i}") for i in range(2)]
    s_gx = [Sem(f"gx{i}") for i in range(4)]
    s_xr = [Sem(f"xr{i}") for i in range(2)]
    s_sm = {(m, sl): Sem(f"st_{m}{sl}")
            for m in ("q", "k", "v", "o", "f") for sl in range(2)}
    s_ini = Sem("ini")
    s_wfree = Sem("wfree")
    s_pe = Sem("pe")
    s_dv = Sem("dv")
    s_ac = Sem("ac")
    all_sems = (s_w + s_xl + s_gx + s_xr + list(s_sm.values())
                + [s_ini, s_wfree, s_pe, s_dv, s_ac])

    def wait(eng, sem, val):
        if sem is None or val is None or val <= 0:
            return
        prog[eng].append(lambda e, s=sem, v=val: e.wait_ge(s.h, v))

    def op(eng, build_fn, incs=()):
        vals = []
        for s, k in incs:
            s.n += k
            vals.append((s, k))

        def run(e, bf=build_fn, vals=tuple(vals)):
            i = bf(e)
            for s, k in vals:
                i = i.then_inc(s.h, k)
        prog[eng].append(run)

    rel = {}

    def set_rel(key, *deps):
        rel[key] = [d for d in deps if d is not None]

    def add_rel(key, dep):
        rel.setdefault(key, []).append(dep)

    def wait_rel(eng, *keys):
        for key in keys:
            for sem, cnt in rel.get(key, ()):
                wait(eng, sem, cnt)

    def vtiny(build_fn):
        op("vector", build_fn, [(s_dv, 1)])
        wait("vector", s_dv, s_dv.n)

    def atiny(build_fn):
        op("scalar", build_fn, [(s_ac, 1)])
        wait("scalar", s_ac, s_ac.n)

    with ExitStack() as ctx:
        e_ = ctx.enter_context

        NRING = 10
        wring = [e_(nc.sbuf_tensor(f"wring{i}", [128, D // 2], CDT)) for i in range(NRING)]
        xln = [e_(nc.sbuf_tensor(f"xln{i}", [128, D], F32)) for i in range(4)]
        xres = [e_(nc.sbuf_tensor(f"xres{i}", [128, D], F32)) for i in range(2)]
        xnT = e_(nc.sbuf_tensor("xnT", [128, NT * T], CDT))
        xn2T = e_(nc.sbuf_tensor("xn2T", [128, NT * T], CDT))
        qat = e_(nc.sbuf_tensor("qat", [128, NT * T], CDT))
        kfT = e_(nc.sbuf_tensor("kfT", [128, NT * T], CDT))
        vg = e_(nc.sbuf_tensor("vg", [128, NT * T], CDT))
        kvstate = e_(nc.sbuf_tensor("kvstate", [128, D], F32))
        kvbf = e_(nc.sbuf_tensor("kvbf", [128, D], CDT))
        ksumst = e_(nc.sbuf_tensor("ksumst", [128, H], F32))
        ksumbf = e_(nc.sbuf_tensor("ksumbf", [128, H], CDT))
        kscur = e_(nc.sbuf_tensor("kscur", [128, 4 * H], F32))
        kftok = e_(nc.sbuf_tensor("kftok", [128, D], CDT))
        vtok = e_(nc.sbuf_tensor("vtok", [128, D], CDT))
        xns_tm = [e_(nc.sbuf_tensor(f"xns{i}", [128, D], CDT)) for i in range(2)]
        a_m = e_(nc.sbuf_tensor("a_m", [128, D], CDT))
        scr = e_(nc.sbuf_tensor("scr", [128, 4096], F32))
        stats = e_(nc.sbuf_tensor("stats", [128, 16], F32))
        lst = e_(nc.sbuf_tensor("lst", [128, 64], F32))
        biases = {m: e_(nc.sbuf_tensor("bs_" + m, [128, NT], F32))
                  for m in ("q", "k", "v", "o")}
        lnw_sb = e_(nc.sbuf_tensor("lnw_sb", [128, NT], F32))
        lnb_sb = e_(nc.sbuf_tensor("lnb_sb", [128, NT], F32))
        identb = e_(nc.sbuf_tensor("identb_s", [128, 128], BF16))
        maskb = e_(nc.sbuf_tensor("maskb_s", [128, 128], BF16))
        onescol_b = e_(nc.sbuf_tensor("onescol_b", [128, 1], BF16))
        epscol_f = e_(nc.sbuf_tensor("epscol_f", [128, 1], F32))
        onescol_f = e_(nc.sbuf_tensor("onescol_f", [128, 1], F32))
        onesrow_f = e_(nc.sbuf_tensor("onesrow_f", [1, 128], F32))
        alpha1 = e_(nc.sbuf_tensor("alpha1", [128, 1], F32))
        sc_sc = e_(nc.sbuf_tensor("sc_sc", [1, 1], F32))
        rscale = e_(nc.sbuf_tensor("rscale", [128, 1], F32))
        pcast = [e_(nc.sbuf_tensor(f"pcast{i}", [128, D], CDT)) for i in range(2)]
        ternb = a_m

        psA = e_(nc.psum_tensor("psA", [128, D], F32))
        psB = e_(nc.psum_tensor("psB", [128, D], F32))
        psBT = psB.ap().bitcast(BF16)

        for s in all_sems:
            s.h = e_(nc.semaphore(s.name))

        SQ_DUMP = (0, 2048)
        EL_E = [(2048, 2560), (2560, 3072)]
        EL_R = [(3072, 3584), (3584, 4096)]
        RZ = (2048, 4096)
        STMP = [(0, 512), (512, 1024)]

        def psBT_lo3d():
            return psBT[:, 0:2048].rearrange("p (t c) -> p t c", c=128)

        op("gpsimd", lambda e: e.dma_start(out=identb[:, :], in_=identb_in[:, :]),
           [(s_ini, 16)])
        op("gpsimd", lambda e: e.dma_start(out=maskb[:, :], in_=maskb_in[:, :]),
           [(s_ini, 16)])
        for m in ("q", "k", "v", "o"):
            op("gpsimd", lambda e, m=m: e.dma_start(out=biases[m][:, :],
                                                    in_=b_in[m][:, :]),
               [(s_ini, 16)])
        op("gpsimd", lambda e: e.dma_start(out=lnw_sb[:, :], in_=lnw_in[:, :]),
           [(s_ini, 16)])
        op("gpsimd", lambda e: e.dma_start(out=lnb_sb[:, :], in_=lnb_in[:, :]),
           [(s_ini, 16)])
        init_gdma = s_ini.n

        wait("scalar", s_ini, init_gdma)
        wait("vector", s_ini, init_gdma)
        wait("tensor", s_ini, init_gdma)

        op("vector", lambda e: e.memset(onescol_b[:, :], 1.0))
        op("vector", lambda e: e.memset(epscol_f[:, :], EPS_AT))
        op("vector", lambda e: e.memset(onescol_f[:, :], 1.0))
        op("vector", lambda e: e.memset(onesrow_f[:, :], 1.0))
        op("vector", lambda e: e.memset(alpha1[:, :], 1.0))
        op("vector", lambda e: e.memset(kvstate[:, :], 0.0))
        op("vector", lambda e: e.memset(kvbf[:, :], 0.0))
        op("vector", lambda e: e.memset(ksumst[:, :], 0.0))
        op("vector", lambda e: e.memset(ksumbf[:, :], 0.0), [(s_dv, 1)])
        init_dv = s_dv.n

        psA_bank_rel = [None] * 4
        pro_steps = []
        cast_rel = [None, None]
        loadslot_rel = [None, None]

        def mk_step_convert(m, r, mode):
            sl = r % 2

            def step():
                lr = loadslot_rel[sl]
                if lr is not None:
                    wait("sync", lr[0], lr[1])
                op("sync", lambda e, m=m, r=r, sl=sl: e.dma_start(
                    out=xres[sl][:, :], in_=w_in[m][r * 128:(r + 1) * 128, :]),
                   [(s_xl[sl], 16)])
                w_loaded = s_xl[sl].n
                wait("vector", s_xl[sl], w_loaded)
                if mode == "absred":
                    op("vector", lambda e, r=r, sl=sl: e.reduce_sum(
                        stats[:, r:r + 1], xres[sl][:, :], axis=AX,
                        apply_absolute_value=True), [(s_dv, 1)])
                    loadslot_rel[sl] = (s_dv, s_dv.n)
                    return
                cslot = pcast[sl]
                cr = cast_rel[sl]
                if cr is not None:
                    wait("vector", cr[0], cr[1])
                if mode == "cast":
                    op("vector", lambda e, sl=sl, cs=cslot: e.tensor_copy(
                        cs[:, :], xres[sl][:, :]), [(s_dv, 1)])
                else:
                    wait("vector", s_dv, s_dv.n)
                    op("vector", lambda e, sl=sl, cs=cslot: e.tensor_scalar(
                        cs[:, :], xres[sl][:, :], rscale[:, :], 0.5,
                        ALU.mult, ALU.is_gt))
                    op("vector", lambda e, sl=sl: e.tensor_scalar(
                        ternb[:, :], xres[sl][:, :], rscale[:, :], -0.5,
                        ALU.mult, ALU.is_lt), [(s_dv, 1)])
                    wait("vector", s_dv, s_dv.n)
                    op("vector", lambda e, cs=cslot: e.tensor_tensor(
                        cs[:, :], cs[:, :], ternb[:, :], ALU.subtract),
                       [(s_dv, 1)])
                    set_rel("a_m", (s_dv, s_dv.n))
                loadslot_rel[sl] = (s_dv, s_dv.n)
                wait("gpsimd", s_dv, s_dv.n)
                for half in range(2):
                    def bf(e, slot=cslot, m=m, r=r, half=half):
                        s2 = slot[:, :].rearrange("p (t d) -> p t d", d=128)
                        s2 = s2[:, half * 8:(half + 1) * 8, :]
                        dst = wsc[MIDX[m], half * 8:(half + 1) * 8, :, r, :]
                        return e.dma_start(out=dst.transpose([1, 0, 2]), in_=s2)
                    op("gpsimd", bf, [(s_sm[(m, sl)], 16)])
                cast_rel[sl] = (s_sm[(m, sl)], s_sm[(m, sl)].n)
            return step

        def mk_step_scale():
            def step():
                wait("vector", s_dv, s_dv.n)
                op("vector", lambda e: e.reduce_sum(
                    scr[:, 0:1], stats[:, 0:NT], axis=AX), [(s_dv, 1)])
                wait("vector", s_dv, s_dv.n)
                wait("tensor", s_dv, s_dv.n)
                r0 = psA_bank_rel[0]
                if r0 is not None:
                    wait("tensor", r0[0], r0[1])
                op("tensor", lambda e: e.matmul(
                    psA[0:1, 0:1], scr[:, 0:1], onescol_f[:, :],
                    start=True, stop=True), [(s_pe, 1)])
                wait("vector", s_pe, s_pe.n)
                vtiny(lambda e: e.tensor_scalar(
                    sc_sc[:, :], psA[0:1, 0:1], 1.0 / (D * D), None, ALU.mult))
                wait("tensor", s_dv, s_dv.n)
                op("tensor", lambda e: e.matmul(
                    psA[:, 1:2], onesrow_f[:, :], sc_sc[:, :],
                    start=True, stop=True), [(s_pe, 1)])
                wait("vector", s_pe, s_pe.n)
                vtiny(lambda e: e.tensor_copy(scr[:, 1:2], psA[:, 1:2]))
                vtiny(lambda e: e.reciprocal(rscale[:, :], scr[:, 1:2]))
                psA_bank_rel[0] = (s_dv, s_dv.n)
            return step

        for m in ("q", "k", "v", "o"):
            for r in range(NT):
                pro_steps.append(mk_step_convert(m, r, "cast"))
        for r in range(NT):
            pro_steps.append(mk_step_convert("f", r, "absred"))
        pro_steps.append(mk_step_scale())
        for r in range(NT):
            pro_steps.append(mk_step_convert("f", r, "tern"))
        PRO_READY = {"q": 16, "k": 32, "v": 48, "o": 64, "f": len(pro_steps)}
        pumped = [0]

        def pump(n):
            while n > 0 and pumped[0] < len(pro_steps):
                pro_steps[pumped[0]]()
                pumped[0] += 1
                n -= 1

        def pump_until(idx):
            pump(max(0, idx - pumped[0]))

        wtile_ctr = [0]
        pending_wfree = [0]

        def stream_whalf(m, to, half):
            k = wtile_ctr[0]
            wtile_ctr[0] += 1
            if k >= NRING:
                wait("sync", s_wfree, k - (NRING - 1))
            if to == 0 and half == 0:
                pump_until(PRO_READY[m])
                wait("sync", s_sm[(m, 0)], s_sm[(m, 0)].n)
                wait("sync", s_sm[(m, 1)], s_sm[(m, 1)].n)
            op("sync", lambda e, m=m, to=to, k=k, half=half: e.dma_start(
                out=wring[k % NRING][:, :].rearrange("p (t d) -> p t d", d=128),
                in_=wsc[MIDX[m], to, :, half * 8:(half + 1) * 8, :]),
               [(s_w[k % NRING], 16)])
            return k, (s_w[k % NRING], s_w[k % NRING].n)

        def load_xln(c):
            sl = c % 4
            wait_rel("gpsimd", f"xln{sl}")
            op("gpsimd", lambda e, c=c, sl=sl: e.dma_start(
                out=xln[sl][:, :], in_=x_in[c * 128:(c + 1) * 128, :]),
               [(s_gx[sl], 16)])
            return (s_gx[sl], s_gx[sl].n)

        def load_xres(c):
            sl = c % 2
            wait_rel("gpsimd", f"xres{sl}")
            op("gpsimd", lambda e, c=c, sl=sl: e.dma_start(
                out=xres[sl][:, :], in_=x_in[c * 128:(c + 1) * 128, :]),
               [(s_xr[sl], 16)])
            return (s_xr[sl], s_xr[sl].n)

        ac_kvbf = 0
        dv_kvbf_state = [0]
        xres_handover = [False]

        pend_ln = {}

        def emit_ln_A(sc2, gate_dv):
            L_SUM, L_SQ, L_MEAN, L_M2, L_VAR, L_U, L_RU, L_W2 = 0, 4, 8, 12, 16, 20, 24, 28
            L_LNU, L_R1, L_LNW2, L_R2, L_NMR, L_SC2, L_NMR2 = 32, 36, 40, 44, 48, 52, 56
            loaded = {}
            for j in range(NCH):
                loaded[j] = load_xln(sc2 * NCH + j)
            ac_sq = 0
            for j in range(NCH):
                xs = xln[(sc2 * NCH + j) % 4]
                wait("scalar", loaded[j][0], loaded[j][1])
                if j == 0 and gate_dv:
                    wait("scalar", s_dv, gate_dv)
                if j > 0:
                    wait("scalar", s_ac, ac_sq)
                op("scalar", lambda e, xs=xs, j=j: e.activation(
                    scr[:, RZ[0]:RZ[1]], xs[:, :], AF.Square,
                    accum_out=lst[:, L_SQ + j:L_SQ + j + 1]), [(s_ac, 1)])
                ac_sq = s_ac.n
                wait("vector", loaded[j][0], loaded[j][1])
                op("vector", lambda e, xs=xs, j=j: e.reduce_sum(
                    lst[:, L_SUM + j:L_SUM + j + 1], xs[:, :], axis=AX),
                   [(s_dv, 1)])
            wait("vector", s_dv, s_dv.n)
            c4 = lambda base: lst[:, base:base + 4]
            vtiny(lambda e: e.tensor_scalar(
                c4(L_MEAN), c4(L_SUM), 1.0 / D, None, ALU.mult))
            vtiny(lambda e: e.tensor_tensor(
                c4(L_M2), c4(L_MEAN), c4(L_MEAN), ALU.mult))
            wait("vector", s_ac, ac_sq)
            vtiny(lambda e: e.scalar_tensor_tensor(
                c4(L_VAR), c4(L_SQ), 1.0 / D, c4(L_M2), ALU.mult, ALU.subtract))
            vtiny(lambda e: e.tensor_scalar(
                c4(L_U), c4(L_VAR), EPS_LN, None, ALU.add))
            vtiny(lambda e: e.reciprocal(c4(L_RU), c4(L_U)))
            vtiny(lambda e: e.tensor_tensor(
                c4(L_W2), c4(L_VAR), c4(L_RU), ALU.mult))
            vtiny(lambda e: e.tensor_scalar(
                c4(L_W2), c4(L_W2), EPS_LN, None, ALU.add))
            wait("scalar", s_dv, s_dv.n)
            atiny(lambda e: e.activation(c4(L_LNU), c4(L_U), AF.Ln))
            atiny(lambda e: e.activation(c4(L_R1), c4(L_LNU), AF.Exp, scale=-0.5))
            atiny(lambda e: e.activation(c4(L_LNW2), c4(L_W2), AF.Ln))
            atiny(lambda e: e.activation(c4(L_R2), c4(L_LNW2), AF.Exp, scale=-0.5))
            wait("vector", s_ac, s_ac.n)
            vtiny(lambda e: e.scalar_tensor_tensor(
                c4(L_NMR), c4(L_MEAN), -1.0, c4(L_R1), ALU.mult, ALU.mult))
            vtiny(lambda e: e.tensor_tensor(
                c4(L_SC2), c4(L_R1), c4(L_R2), ALU.mult))
            vtiny(lambda e: e.scalar_tensor_tensor(
                c4(L_NMR2), c4(L_MEAN), -1.0, c4(L_SC2), ALU.mult, ALU.mult))
            dv_aff = s_dv.n
            wait("scalar", s_dv, dv_aff)
            ac_applies = {}
            for j in range(2):
                c = sc2 * NCH + j
                xs = xln[c % 4]
                xntm = [kftok, vtok][j % 2]
                xnstm = xns_tm[j % 2]
                wait_rel("scalar", "kftok" if j % 2 == 0 else "vtok")
                op("scalar", lambda e, xs=xs, xm=xntm, j=j: e.activation(
                    xm[:, :], xs[:, :], AF.Identity,
                    bias=lst[:, L_NMR + j:L_NMR + j + 1],
                    scale=lst[:, L_R1 + j:L_R1 + j + 1]), [(s_ac, 1)])
                wait_rel("scalar", f"xns{j % 2}")
                op("scalar", lambda e, xs=xs, xm=xnstm, j=j: e.activation(
                    xm[:, :], xs[:, :], AF.Identity,
                    bias=lst[:, L_NMR2 + j:L_NMR2 + j + 1],
                    scale=lst[:, L_SC2 + j:L_SC2 + j + 1]), [(s_ac, 1)])
                ac_applies[j] = s_ac.n
                set_rel(f"xln{c % 4}", (s_ac, s_ac.n))
            pend_ln[sc2] = (ac_applies, s_dv.n)

        def emit_apply(sc2, j, ac_applies):
            L_R1, L_NMR, L_SC2, L_NMR2 = 36, 48, 52, 56
            c = sc2 * NCH + j
            xs = xln[c % 4]
            xntm = [kftok, vtok][j % 2]
            xnstm = xns_tm[j % 2]
            wait_rel("scalar", "kftok" if j % 2 == 0 else "vtok")
            op("scalar", lambda e, xs=xs, xm=xntm, j=j: e.activation(
                xm[:, :], xs[:, :], AF.Identity,
                bias=lst[:, L_NMR + j:L_NMR + j + 1],
                scale=lst[:, L_R1 + j:L_R1 + j + 1]), [(s_ac, 1)])
            wait_rel("scalar", f"xns{j % 2}")
            op("scalar", lambda e, xs=xs, xm=xnstm, j=j: e.activation(
                xm[:, :], xs[:, :], AF.Identity,
                bias=lst[:, L_NMR2 + j:L_NMR2 + j + 1],
                scale=lst[:, L_SC2 + j:L_SC2 + j + 1]), [(s_ac, 1)])
            ac_applies[j] = s_ac.n
            set_rel(f"xln{c % 4}", (s_ac, s_ac.n))

        def emit_ln_B(sc2):
            ac_applies, dv_aff = pend_ln.pop(sc2)
            for j in range(NCH):
                xntm = [kftok, vtok][j % 2]
                xnstm = xns_tm[j % 2]
                wait("tensor", s_ac, ac_applies[j])
                wait("tensor", s_ini, init_gdma)
                wait_rel("tensor", "psB")
                for t in range(NT):
                    incs = [(s_pe, 1)] if t == NT - 1 else []
                    op("tensor", lambda e, t=t, xm=xnstm: e.transpose(
                        psBT[:, 2048 + t * 128:2048 + (t + 1) * 128],
                        xm[:, t * 128:(t + 1) * 128], identb[:, :]), incs)
                pe_tr2 = s_pe.n
                for t in range(NT):
                    incs = [(s_pe, 1)] if t == NT - 1 else []
                    op("tensor", lambda e, t=t, xm=xntm: e.transpose(
                        psBT[:, t * 128:(t + 1) * 128],
                        xm[:, t * 128:(t + 1) * 128], identb[:, :]), incs)
                pe_tr1 = s_pe.n
                set_rel("kftok" if j % 2 == 0 else "vtok", (s_pe, pe_tr1))
                set_rel(f"xns{j % 2}", (s_pe, pe_tr2))
                if j + 2 < NCH:
                    emit_apply(sc2, j + 2, ac_applies)
                wait("vector", s_pe, pe_tr1)
                wait_rel("vector", "xnT")
                op("vector", lambda e, j=j: e.tensor_copy(
                    bass.AP(xnT, j * 128, [[NT * T, 128], [T, NT], [1, 128]]),
                    psBT_lo3d()), [(s_dv, 1)])
                dv_xnT = s_dv.n
                wait("vector", s_pe, pe_tr2)
                wait_rel("vector", "xn2T")
                for t in range(NT):
                    incs = [(s_dv, 1)] if t == NT - 1 else []
                    op("vector", lambda e, t=t, j=j: e.tensor_scalar(
                        xn2T[:, t * T + j * 128: t * T + (j + 1) * 128],
                        psBT[:, 2048 + t * 128:2048 + (t + 1) * 128],
                        lnw_sb[:, t:t + 1], lnb_sb[:, t:t + 1],
                        ALU.mult, ALU.add), incs)
                set_rel("psB", (s_dv, s_dv.n))
                pump(2)
            return s_dv.n, s_ac.n

        for sc in range(nsc):
            if sc == 0:
                emit_ln_A(0, 0)
            dv_ln_done, ac_ln_done = emit_ln_B(sc)

            def proj_tile(m, to, evac_fn, rhs_buf, rhs_deps, ps=None):
                ps_t = psA if ps is None else ps
                kA, wnA = stream_whalf(m, to, 0)
                kB, wnB = stream_whalf(m, to, 1)
                bk = to % 4
                if ps is None:
                    r0 = psA_bank_rel[bk]
                    if r0 is not None:
                        wait("tensor", r0[0], r0[1])
                if to == 0:
                    for sem, cnt in rhs_deps:
                        wait("tensor", sem, cnt)
                wait("tensor", wnA[0], wnA[1])
                for ti in range(NT):
                    k = kA if ti < 8 else kB
                    incs = []
                    if ti == 0 and pending_wfree[0]:
                        incs = [(s_wfree, pending_wfree[0])]
                        pending_wfree[0] = 0
                    if ti == 8:
                        incs = [(s_wfree, 1)]
                        wait("tensor", wnB[0], wnB[1])
                    if ti == NT - 1:
                        incs = [(s_pe, 1)]
                    op("tensor", lambda e, k=k, ti=ti, bk=bk, rb=rhs_buf, pt=ps_t:
                       e.matmul(
                           pt[:, bk * 512:(bk + 1) * 512],
                           wring[k % NRING][:, (ti % 8) * 128:(ti % 8 + 1) * 128],
                           rb[:, ti * T:(ti + 1) * T],
                           start=(ti == 0), stop=(ti == NT - 1)), incs)
                pending_wfree[0] += 1
                evac_fn(to, bk, s_pe.n)
                pump(2)

            def proj_phase(m, evac_fn, rhs_buf, rhs_deps):
                for to in range(NT):
                    proj_tile(m, to, evac_fn, rhs_buf, rhs_deps)

            def mk_evac_qk(dst, bias_key):
                def evac(to, bk, pe_n):
                    esl = EL_E[to % 2]
                    rsl = EL_R[to % 2]
                    wait("scalar", s_pe, pe_n)
                    wait_rel("scalar", f"el{to % 2}")
                    op("scalar", lambda e, bk=bk, to=to, esl=esl: e.activation(
                        scr[:, esl[0]:esl[1]], psA[:, bk * 512:(bk + 1) * 512],
                        AF.Exp, bias=biases[bias_key][:, to:to + 1]), [(s_ac, 1)])
                    op("scalar", lambda e, bk=bk, to=to, rsl=rsl: e.activation(
                        scr[:, rsl[0]:rsl[1]], psA[:, bk * 512:(bk + 1) * 512],
                        AF.Relu, bias=biases[bias_key][:, to:to + 1]), [(s_ac, 1)])
                    psA_bank_rel[bk] = (s_ac, s_ac.n)
                    wait("vector", s_ac, s_ac.n)
                    op("vector", lambda e, to=to, esl=esl, rsl=rsl:
                       e.scalar_tensor_tensor(
                           dst[:, to * T:(to + 1) * T], scr[:, esl[0]:esl[1]], 1.0,
                           scr[:, rsl[0]:rsl[1]], ALU.min, ALU.add), [(s_dv, 1)])
                    set_rel(f"el{to % 2}", (s_dv, s_dv.n))
                return evac

            proj_phase("q", mk_evac_qk(qat, "q"), xnT, [(s_dv, dv_ln_done)])
            q_done_dv = s_dv.n
            proj_phase("k", mk_evac_qk(kfT, "k"), xnT, [(s_dv, dv_ln_done)])
            k_done_dv = s_dv.n
            wait("vector", s_dv, s_dv.n)
            for j in range(NCH):
                op("vector", lambda e, j=j: e.reduce_sum(
                    bass.AP(kscur, j * H, [[4 * H, 128], [1, H], [0, 1]]),
                    bass.AP(kfT, j * 128, [[NT * T, 128], [T, H], [1, 128]]),
                    axis=AX), [(s_dv, 1)])
            dv_ksum4 = s_dv.n

            def evac_v(to, bk, pe_n):
                wait("scalar", s_pe, pe_n)
                op("scalar", lambda e, to=to, bk=bk: e.activation(
                    vg[:, to * T:(to + 1) * T], psA[:, bk * 512:(bk + 1) * 512],
                    AF.Identity, bias=biases["v"][:, to:to + 1],
                    scale=1.0), [(s_ac, 1)])
                psA_bank_rel[bk] = (s_ac, s_ac.n)

            proj_phase("v", evac_v, xnT, [(s_dv, dv_ln_done)])
            v_done_ac = s_ac.n

            def evac_f(to, bk, pe_n):
                wait("scalar", s_pe, pe_n)
                op("scalar", lambda e, to=to, bk=bk: e.activation(
                    xnT[:, to * T:(to + 1) * T], psB[:, bk * 512:(bk + 1) * 512],
                    AF.Silu), [(s_ac, 1)])
                add_rel("psB", (s_ac, s_ac.n))

            for j in range(NCH):
                wait_rel("tensor", "psB")
                wait("tensor", s_dv, k_done_dv)
                wait("tensor", s_ac, v_done_ac)
                for h in range(H):
                    incs = [(s_pe, 1)] if h == H - 1 else []
                    op("tensor", lambda e, h=h, j=j: e.transpose(
                        psBT[:, h * 128:(h + 1) * 128],
                        kfT[:, h * T + j * 128: h * T + (j + 1) * 128],
                        identb[:, :]), incs)
                pe_kft = s_pe.n
                for h in range(H):
                    incs = [(s_pe, 1)] if h == H - 1 else []
                    op("tensor", lambda e, h=h, j=j: e.transpose(
                        psBT[:, 2048 + h * 128:2048 + (h + 1) * 128],
                        vg[:, h * T + j * 128: h * T + (j + 1) * 128],
                        identb[:, :]), incs)
                pe_vt = s_pe.n
                wait("scalar", s_pe, pe_kft)
                wait_rel("scalar", "kftok")
                op("scalar", lambda e: e.copy(
                    kftok[:, :], psBT[:, 0:2048]), [(s_ac, 1)])
                ac_kftok = s_ac.n
                wait("scalar", s_pe, pe_vt)
                wait_rel("scalar", "vtok")
                op("scalar", lambda e: e.copy(
                    vtok[:, :], psBT[:, 2048:4096]), [(s_ac, 1)])
                ac_vtok = s_ac.n
                wait("tensor", s_dv, q_done_dv)
                for bk in range(4):
                    r0 = psA_bank_rel[bk]
                    if r0 is not None:
                        wait("tensor", r0[0], r0[1])
                for h in range(H):
                    incs = [(s_pe, 1)] if h == H - 1 else []
                    op("tensor", lambda e, h=h, j=j: e.matmul(
                        psA[:, h * 128:(h + 1) * 128],
                        kfT[:, h * T + j * 128: h * T + (j + 1) * 128],
                        qat[:, h * T + j * 128: h * T + (j + 1) * 128],
                        start=True, stop=True), incs)
                pe_a = s_pe.n
                wait("vector", s_pe, pe_a)
                wait_rel("vector", "a_m")
                op("vector", lambda e: e.tensor_tensor(
                    a_m[:, 0:1024].rearrange("p (t c) -> p t c", c=128),
                    psA[:, 0:1024].rearrange("p (t c) -> p t c", c=128),
                    maskb[:, :].unsqueeze(1).broadcast_to([128, H // 2, 128]),
                    ALU.mult), [(s_dv, 1)])
                dv_am_a = s_dv.n
                op("vector", lambda e: e.tensor_tensor(
                    a_m[:, 1024:2048].rearrange("p (t c) -> p t c", c=128),
                    psA[:, 1024:2048].rearrange("p (t c) -> p t c", c=128),
                    maskb[:, :].unsqueeze(1).broadcast_to([128, H // 2, 128]),
                    ALU.mult), [(s_dv, 1)])
                dv_am_b = s_dv.n
                wait("tensor", s_ac, ac_vtok)
                wait("tensor", s_ac, ac_kftok)
                wait("tensor", s_dv, dv_kvbf_state[0])
                for h in range(H):
                    if h == 0:
                        wait("tensor", s_dv, dv_am_a)
                    if h == H // 2:
                        wait("tensor", s_dv, dv_am_b)
                    op("tensor", lambda e, h=h, j=j: e.matmul(
                        psB[:, h * 128:(h + 1) * 128],
                        kvbf[:, h * 128:(h + 1) * 128],
                        qat[:, h * T + j * 128: h * T + (j + 1) * 128],
                        start=True, stop=False))
                    incs = [(s_pe, 1)] if h == H - 1 else []
                    op("tensor", lambda e, h=h: e.matmul(
                        psB[:, h * 128:(h + 1) * 128],
                        vtok[:, h * 128:(h + 1) * 128],
                        a_m[:, h * 128:(h + 1) * 128],
                        start=False, stop=True), incs)
                pe_io = s_pe.n
                for h in range(H):
                    op("tensor", lambda e, h=h, j=j: e.matmul(
                        psA[:, h * 128:(h + 1) * 128],
                        bass.AP(ksumbf, h, [[H, 128], [0, 128]]),
                        qat[:, h * T + j * 128: h * T + (j + 1) * 128],
                        start=True, stop=False))
                    incs = [(s_pe, 1)] if h == H - 1 else []
                    op("tensor", lambda e, h=h: e.matmul(
                        psA[:, h * 128:(h + 1) * 128],
                        bass.AP(onescol_b, 0, [[1, 128], [0, 128]]),
                        a_m[:, h * 128:(h + 1) * 128],
                        start=False, stop=True), incs)
                pe_z = s_pe.n
                wait("scalar", s_pe, pe_z)
                op("scalar", lambda e: e.activation(
                    scr[:, SQ_DUMP[0]:SQ_DUMP[1]], psA[:, :], AF.Ln,
                    bias=epscol_f[:, :]), [(s_ac, 1)])
                ac_lnz = s_ac.n
                wait("scalar", s_ac, ac_lnz)
                op("scalar", lambda e: e.activation(
                    scr[:, RZ[0]:RZ[1]], scr[:, SQ_DUMP[0]:SQ_DUMP[1]],
                    AF.Exp, scale=-1.0), [(s_ac, 1)])
                ac_rz = s_ac.n
                wait("tensor", s_ac, ac_lnz)
                for h in range(H):
                    incs = [(s_pe, 1)] if h == H - 1 else []
                    op("tensor", lambda e, h=h: e.matmul(
                        psA[:, h * 128:(h + 1) * 128],
                        kftok[:, h * 128:(h + 1) * 128],
                        vtok[:, h * 128:(h + 1) * 128],
                        start=True, stop=True), incs)
                pe_kv = s_pe.n
                wait("vector", s_pe, pe_io)
                wait("vector", s_ac, ac_rz)
                op("vector", lambda e, j=j: e.tensor_tensor(
                    bass.AP(qat, j * 128, [[NT * T, 128], [T, H], [1, 128]]),
                    psB[:, :].rearrange("p (t c) -> p t c", c=128),
                    scr[:, RZ[0]:RZ[1]].rearrange("p (t c) -> p t c", c=128),
                    ALU.mult), [(s_dv, 1)])
                set_rel("psB", (s_dv, s_dv.n), (s_ac, ac_vtok))
                wait("vector", s_pe, pe_kv)
                op("vector", lambda e: e.tensor_tensor(
                    kvbf[:, :], kvstate[:, :], psA[:, :], ALU.add), [(s_dv, 1)])
                dv_kvbf = s_dv.n
                wait("vector", s_dv, dv_ksum4)
                op("vector", lambda e, j=j: e.tensor_tensor(
                    ksumbf[:, :], ksumst[:, :],
                    kscur[:, j * H:(j + 1) * H], ALU.add), [(s_dv, 1)])
                dv_ksbf = s_dv.n
                op("vector", lambda e: e.tensor_tensor(
                    kvstate[:, :], kvstate[:, :], psA[:, :], ALU.add), [(s_dv, 1)])
                dv_kvupd = s_dv.n
                for bk in range(4):
                    psA_bank_rel[bk] = (s_dv, dv_kvupd)
                wait("vector", s_dv, dv_ksbf)
                vtiny(lambda e, j=j: e.tensor_tensor(
                    ksumst[:, :], ksumst[:, :],
                    kscur[:, j * H:(j + 1) * H], ALU.add))
                dv_kvbf_state[0] = dv_kvbf
                set_rel("kftok", (s_pe, pe_kv))
                set_rel("vtok", (s_pe, pe_kv))
                set_rel("a_m", (s_pe, pe_io))
                wait_rel("tensor", "psB")
                set_rel("psB")
                for to in range(4 * j, 4 * j + 4):
                    proj_tile("f", to, evac_f, xn2T,
                              [(s_ac, ac_ln_done)], ps=psB)
            f_done_ac = s_ac.n
            attn_dv_done = s_dv.n
            set_rel("el0", (s_dv, attn_dv_done))
            set_rel("el1", (s_dv, attn_dv_done))

            if sc + 1 < nsc:
                emit_ln_A(sc + 1, attn_dv_done)

            if not xres_handover[0]:
                pump_until(len(pro_steps))
                for sl2 in range(2):
                    if loadslot_rel[sl2] is not None:
                        set_rel(f"xres{sl2}", loadslot_rel[sl2])
                xres_handover[0] = True
            xres_loaded = {}
            xres_loaded[0] = load_xres(sc * NCH + 0)
            xres_loaded[1] = load_xres(sc * NCH + 1)

            dv_sg = {}

            def evac_o(to, bk, pe_n):
                ssl = STMP[to % 2]
                wait("scalar", s_pe, pe_n)
                wait_rel("scalar", f"st{to % 2}")
                op("scalar", lambda e, to=to, bk=bk, ssl=ssl: e.activation(
                    scr[:, ssl[0]:ssl[1]], psA[:, bk * 512:(bk + 1) * 512],
                    AF.Identity, bias=biases["o"][:, to:to + 1], scale=1.0,
                    ), [(s_ac, 1)])
                psA_bank_rel[bk] = (s_ac, s_ac.n)
                wait("vector", s_ac, s_ac.n)
                if to == 0:
                    wait("vector", s_ac, f_done_ac)
                op("vector", lambda e, to=to, ssl=ssl: e.tensor_tensor(
                    vg[:, to * T:(to + 1) * T], scr[:, ssl[0]:ssl[1]],
                    xnT[:, to * T:(to + 1) * T], ALU.mult), [(s_dv, 1)])
                dv_sg[to] = s_dv.n
                set_rel(f"st{to % 2}", (s_dv, s_dv.n))

            proj_phase("o", evac_o, qat, [(s_dv, attn_dv_done)])
            sg_dv_done = s_dv.n

            for j in range(NCH):
                c = sc * NCH + j
                wait_rel("tensor", "psB")
                for t in range(NT):
                    wait("tensor", s_dv, dv_sg[t])
                    incs = [(s_pe, 1)] if t == NT - 1 else []
                    op("tensor", lambda e, t=t, j=j: e.transpose(
                        psBT[:, t * 128:(t + 1) * 128],
                        vg[:, t * T + j * 128: t * T + (j + 1) * 128],
                        identb[:, :]), incs)
                pe_sg = s_pe.n
                sl = c % 2
                wait("vector", s_pe, pe_sg)
                wait("vector", xres_loaded[j][0], xres_loaded[j][1])
                op("vector", lambda e, sl=sl: e.tensor_tensor(
                    xres[sl][:, :].rearrange("p (t d) -> p t d", d=128),
                    psBT_lo3d(),
                    xres[sl][:, :].rearrange("p (t d) -> p t d", d=128),
                    ALU.add), [(s_dv, 1)])
                dv_out = s_dv.n
                set_rel("psB", (s_dv, dv_out))
                wait("gpsimd", s_dv, dv_out)
                op("gpsimd", lambda e, c=c, sl=sl: e.dma_start(
                    out=y_out[c * 128:(c + 1) * 128, :], in_=xres[sl][:, :]),
                   [(s_xr[sl], 16)])
                set_rel(f"xres{sl}", (s_xr[sl], s_xr[sl].n))
                if j + 2 < NCH:
                    xres_loaded[j + 2] = load_xres(sc * NCH + j + 2)
            set_rel("xnT", (s_dv, sg_dv_done))
            set_rel("vg_next", (s_pe, s_pe.n))

        wait("gpsimd", s_xr[0], s_xr[0].n)
        wait("gpsimd", s_xr[1], s_xr[1].n)

        if debug:
            dbg_specs = [
                ("xnT", xnT, CDT), ("xn2T", xn2T, CDT), ("qat", qat, CDT),
                ("kfT", kfT, CDT), ("vg", vg, CDT), ("kvstate", kvstate, F32),
                ("stats", stats, F32), ("rscale", rscale, F32),
                ("kftok", kftok, CDT), ("vtok", vtok, CDT), ("a_m", a_m, CDT),
                ("ksumst", ksumst, F32), ("scr", scr, F32),
                ("kvbf", kvbf, CDT),
            ]
            wait("gpsimd", s_dv, s_dv.n)
            wait("gpsimd", s_ac, s_ac.n)
            wait("gpsimd", s_pe, s_pe.n)
            for name, buf, dt in dbg_specs:
                dbg_out = nc.dram_tensor("dbg_" + name, list(buf.shape), dt,
                                         kind="ExternalOutput")
                op("gpsimd", lambda e, o=dbg_out, b=buf: e.dma_start(
                    out=o[:, :], in_=b[:, :]), [(s_ini, 16)])
            wait("gpsimd", s_ini, s_ini.n)

        block = e_(nc.Block())

        @block.sync
        def _(sync):
            for fn in prog["sync"]:
                fn(sync)

        @block.gpsimd
        def _(gpsimd):
            for fn in prog["gpsimd"]:
                fn(gpsimd)

        @block.scalar
        def _(scalar):
            for fn in prog["scalar"]:
                fn(scalar)

        @block.vector
        def _(vector):
            for fn in prog["vector"]:
                fn(vector)

        @block.tensor
        def _(tensor):
            for fn in prog["tensor"]:
                fn(tensor)

    return nc


_NC_CACHE = {}


def _get_nc(nsc):
    if nsc not in _NC_CACHE:
        _NC_CACHE[nsc] = build_nc(nsc)
    return _NC_CACHE[nsc]


def kernel(L0_attn_x, L0_attn_Wq, L0_attn_bq, L0_attn_Wk, L0_attn_bk,
           L0_attn_Wv, L0_attn_bv, L0_attn_Wo, L0_attn_bo,
           L0_ffn_W, L0_ffn_ln_w, L0_ffn_ln_b):
    nc = _get_nc(S // T)
    x = np.ascontiguousarray(np.asarray(L0_attn_x, dtype=np.float32))

    def colmajor16(v):
        return np.ascontiguousarray(
            np.asarray(v, dtype=np.float32).reshape(NT, 128).T)

    common = {
        "w_q": np.ascontiguousarray(np.asarray(L0_attn_Wq, dtype=np.float32)),
        "w_k": np.ascontiguousarray(np.asarray(L0_attn_Wk, dtype=np.float32)),
        "w_v": np.ascontiguousarray(np.asarray(L0_attn_Wv, dtype=np.float32)),
        "w_o": np.ascontiguousarray(np.asarray(L0_attn_Wo, dtype=np.float32)),
        "w_f": np.ascontiguousarray(np.asarray(L0_ffn_W, dtype=np.float32)),
        "b_q": colmajor16(L0_attn_bq),
        "b_k": colmajor16(L0_attn_bk),
        "b_v": colmajor16(L0_attn_bv),
        "b_o": colmajor16(L0_attn_bo),
        "lnw": colmajor16(L0_ffn_ln_w),
        "lnb": colmajor16(L0_ffn_ln_b),
        "identb": np.eye(128, dtype=ml_dtypes.bfloat16),
        "maskb": np.triu(np.ones((128, 128), dtype=ml_dtypes.bfloat16)),
    }
    in_maps = []
    for b in range(N_CORES):
        m = dict(common)
        m["x"] = np.ascontiguousarray(x[b])
        in_maps.append(m)
    res = run_bass_kernel_spmd(nc, in_maps, core_ids=list(range(N_CORES)))
    out = np.stack([res.results[b]["y"] for b in range(N_CORES)], axis=0)
    return out


# version 13
# speedup vs baseline: 1.0567x; 1.0266x over previous
import numpy as np
import ml_dtypes
from contextlib import ExitStack

import concourse.bass as bass
import concourse.mybir as mybir
from concourse.bass_utils import run_bass_kernel_spmd

F32 = mybir.dt.float32
BF16 = mybir.dt.bfloat16
CDT = BF16
AF = mybir.ActivationFunctionType
ALU = mybir.AluOpType
AX = mybir.AxisListType.X

S = 4096
D = 2048
H = 16
C = 128
T = 512
NCH = T // C
NT = D // 128
EPS_LN = 1e-5
EPS_AT = 1e-6
N_CORES = 8


class Sem:
    def __init__(self, name):
        self.name = name
        self.n = 0
        self.h = None


def build_nc(nsc=S // T, debug=False):
    nc = bass.Bass("TRN2", target_bir_lowering=False)

    x_in = nc.dram_tensor("x", [S, D], F32, kind="ExternalInput")
    y_out = nc.dram_tensor("y", [S, D], F32, kind="ExternalOutput")
    w_in = {m: nc.dram_tensor("w_" + m, [D, D], F32, kind="ExternalInput")
            for m in ("q", "k", "v", "o", "f")}
    b_in = {m: nc.dram_tensor("b_" + m, [128, NT], F32, kind="ExternalInput")
            for m in ("q", "k", "v", "o")}
    lnw_in = nc.dram_tensor("lnw", [128, NT], F32, kind="ExternalInput")
    lnb_in = nc.dram_tensor("lnb", [128, NT], F32, kind="ExternalInput")
    identb_in = nc.dram_tensor("identb", [128, 128], BF16, kind="ExternalInput")
    maskb_in = nc.dram_tensor("maskb", [128, 128], BF16, kind="ExternalInput")
    wsc = nc.dram_tensor("wsc", [5, NT, 128, NT, 128], CDT, kind="Internal")
    MIDX = {"q": 0, "k": 1, "v": 2, "o": 3, "f": 4}

    prog = {k: [] for k in ("sync", "gpsimd", "scalar", "vector", "tensor")}
    s_w = [Sem(f"w{i}") for i in range(10)]
    s_xl = [Sem(f"xl{i}") for i in range(2)]
    s_gx = [Sem(f"gx{i}") for i in range(4)]
    s_xr = [Sem(f"xr{i}") for i in range(2)]
    s_sm = {(m, sl): Sem(f"st_{m}{sl}")
            for m in ("q", "k", "v", "o", "f") for sl in range(2)}
    s_ini = Sem("ini")
    s_wfree = Sem("wfree")
    s_pe = Sem("pe")
    s_dv = Sem("dv")
    s_ac = Sem("ac")
    all_sems = (s_w + s_xl + s_gx + s_xr + list(s_sm.values())
                + [s_ini, s_wfree, s_pe, s_dv, s_ac])

    def wait(eng, sem, val):
        if sem is None or val is None or val <= 0:
            return
        prog[eng].append(lambda e, s=sem, v=val: e.wait_ge(s.h, v))

    def op(eng, build_fn, incs=()):
        vals = []
        for s, k in incs:
            s.n += k
            vals.append((s, k))

        def run(e, bf=build_fn, vals=tuple(vals)):
            i = bf(e)
            for s, k in vals:
                i = i.then_inc(s.h, k)
        prog[eng].append(run)

    rel = {}

    def set_rel(key, *deps):
        rel[key] = [d for d in deps if d is not None]

    def add_rel(key, dep):
        rel.setdefault(key, []).append(dep)

    def wait_rel(eng, *keys):
        for key in keys:
            for sem, cnt in rel.get(key, ()):
                wait(eng, sem, cnt)

    def vtiny(build_fn):
        op("vector", build_fn, [(s_dv, 1)])
        wait("vector", s_dv, s_dv.n)

    def atiny(build_fn):
        op("scalar", build_fn, [(s_ac, 1)])
        wait("scalar", s_ac, s_ac.n)

    with ExitStack() as ctx:
        e_ = ctx.enter_context

        NRING = 10
        wring = [e_(nc.sbuf_tensor(f"wring{i}", [128, D // 2], CDT)) for i in range(NRING)]
        xln = [e_(nc.sbuf_tensor(f"xln{i}", [128, D], F32)) for i in range(4)]
        xres = [e_(nc.sbuf_tensor(f"xres{i}", [128, D], F32)) for i in range(2)]
        xnT = e_(nc.sbuf_tensor("xnT", [128, NT * T], CDT))
        xn2T = e_(nc.sbuf_tensor("xn2T", [128, NT * T], CDT))
        qat = e_(nc.sbuf_tensor("qat", [128, NT * T], CDT))
        kfT = e_(nc.sbuf_tensor("kfT", [128, NT * T], CDT))
        vg = e_(nc.sbuf_tensor("vg", [128, NT * T], CDT))
        kvstate = e_(nc.sbuf_tensor("kvstate", [128, D], F32))
        kvbf = e_(nc.sbuf_tensor("kvbf", [128, D], CDT))
        ksumst = e_(nc.sbuf_tensor("ksumst", [128, H], F32))
        ksumbf = e_(nc.sbuf_tensor("ksumbf", [128, H], CDT))
        kscur = e_(nc.sbuf_tensor("kscur", [128, 4 * H], F32))
        kftok = e_(nc.sbuf_tensor("kftok", [128, D], CDT))
        vtok = e_(nc.sbuf_tensor("vtok", [128, D], CDT))
        xns_tm = [e_(nc.sbuf_tensor(f"xns{i}", [128, D], CDT)) for i in range(2)]
        a_m = e_(nc.sbuf_tensor("a_m", [128, D], CDT))
        scr = e_(nc.sbuf_tensor("scr", [128, 4096], F32))
        stats = e_(nc.sbuf_tensor("stats", [128, 16], F32))
        lst = e_(nc.sbuf_tensor("lst", [128, 64], F32))
        biases = {m: e_(nc.sbuf_tensor("bs_" + m, [128, NT], F32))
                  for m in ("q", "k", "v", "o")}
        lnw_sb = e_(nc.sbuf_tensor("lnw_sb", [128, NT], F32))
        lnb_sb = e_(nc.sbuf_tensor("lnb_sb", [128, NT], F32))
        identb = e_(nc.sbuf_tensor("identb_s", [128, 128], BF16))
        maskb = e_(nc.sbuf_tensor("maskb_s", [128, 128], BF16))
        onescol_b = e_(nc.sbuf_tensor("onescol_b", [128, 1], BF16))
        epscol_f = e_(nc.sbuf_tensor("epscol_f", [128, 1], F32))
        onescol_f = e_(nc.sbuf_tensor("onescol_f", [128, 1], F32))
        onesrow_f = e_(nc.sbuf_tensor("onesrow_f", [1, 128], F32))
        alpha1 = e_(nc.sbuf_tensor("alpha1", [128, 1], F32))
        sc_sc = e_(nc.sbuf_tensor("sc_sc", [1, 1], F32))
        rscale = e_(nc.sbuf_tensor("rscale", [128, 1], F32))
        pcast = [e_(nc.sbuf_tensor(f"pcast{i}", [128, D], CDT)) for i in range(2)]
        ternb = a_m

        psA = e_(nc.psum_tensor("psA", [128, D], F32))
        psB = e_(nc.psum_tensor("psB", [128, D], F32))
        psBT = psB.ap().bitcast(BF16)

        for s in all_sems:
            s.h = e_(nc.semaphore(s.name))

        SQ_DUMP = (0, 2048)
        EL_E = [(2048, 2560), (2560, 3072)]
        EL_R = [(3072, 3584), (3584, 4096)]
        RZ = (2048, 4096)
        STMP = [(0, 512), (512, 1024)]

        def psBT_lo3d():
            return psBT[:, 0:2048].rearrange("p (t c) -> p t c", c=128)

        op("gpsimd", lambda e: e.dma_start(out=identb[:, :], in_=identb_in[:, :]),
           [(s_ini, 16)])
        op("gpsimd", lambda e: e.dma_start(out=maskb[:, :], in_=maskb_in[:, :]),
           [(s_ini, 16)])
        for m in ("q", "k", "v", "o"):
            op("gpsimd", lambda e, m=m: e.dma_start(out=biases[m][:, :],
                                                    in_=b_in[m][:, :]),
               [(s_ini, 16)])
        op("gpsimd", lambda e: e.dma_start(out=lnw_sb[:, :], in_=lnw_in[:, :]),
           [(s_ini, 16)])
        op("gpsimd", lambda e: e.dma_start(out=lnb_sb[:, :], in_=lnb_in[:, :]),
           [(s_ini, 16)])
        init_gdma = s_ini.n

        wait("scalar", s_ini, init_gdma)
        wait("vector", s_ini, init_gdma)
        wait("tensor", s_ini, init_gdma)

        op("vector", lambda e: e.memset(onescol_b[:, :], 1.0))
        op("vector", lambda e: e.memset(epscol_f[:, :], EPS_AT))
        op("vector", lambda e: e.memset(onescol_f[:, :], 1.0))
        op("vector", lambda e: e.memset(onesrow_f[:, :], 1.0))
        op("vector", lambda e: e.memset(alpha1[:, :], 1.0))
        op("vector", lambda e: e.memset(kvstate[:, :], 0.0))
        op("vector", lambda e: e.memset(kvbf[:, :], 0.0))
        op("vector", lambda e: e.memset(ksumst[:, :], 0.0))
        op("vector", lambda e: e.memset(ksumbf[:, :], 0.0), [(s_dv, 1)])
        init_dv = s_dv.n

        psA_bank_rel = [None] * 4
        pro_steps = []
        cast_rel = [None, None]
        loadslot_rel = [None, None]

        def mk_step_convert(m, r, mode):
            sl = r % 2

            def step():
                lr = loadslot_rel[sl]
                if lr is not None:
                    wait("sync", lr[0], lr[1])
                op("sync", lambda e, m=m, r=r, sl=sl: e.dma_start(
                    out=xres[sl][:, :], in_=w_in[m][r * 128:(r + 1) * 128, :]),
                   [(s_xl[sl], 16)])
                w_loaded = s_xl[sl].n
                wait("vector", s_xl[sl], w_loaded)
                if mode == "absred":
                    op("vector", lambda e, r=r, sl=sl: e.reduce_sum(
                        stats[:, r:r + 1], xres[sl][:, :], axis=AX,
                        apply_absolute_value=True), [(s_dv, 1)])
                    loadslot_rel[sl] = (s_dv, s_dv.n)
                    return
                cslot = pcast[sl]
                cr = cast_rel[sl]
                if cr is not None:
                    wait("vector", cr[0], cr[1])
                if mode == "cast":
                    op("vector", lambda e, sl=sl, cs=cslot: e.tensor_copy(
                        cs[:, :], xres[sl][:, :]), [(s_dv, 1)])
                else:
                    wait("vector", s_dv, s_dv.n)
                    op("vector", lambda e, sl=sl, cs=cslot: e.tensor_scalar(
                        cs[:, :], xres[sl][:, :], rscale[:, :], 0.5,
                        ALU.mult, ALU.is_gt))
                    op("vector", lambda e, sl=sl: e.tensor_scalar(
                        ternb[:, :], xres[sl][:, :], rscale[:, :], -0.5,
                        ALU.mult, ALU.is_lt), [(s_dv, 1)])
                    wait("vector", s_dv, s_dv.n)
                    op("vector", lambda e, cs=cslot: e.tensor_tensor(
                        cs[:, :], cs[:, :], ternb[:, :], ALU.subtract),
                       [(s_dv, 1)])
                    set_rel("a_m", (s_dv, s_dv.n))
                loadslot_rel[sl] = (s_dv, s_dv.n)
                wait("gpsimd", s_dv, s_dv.n)
                for half in range(2):
                    def bf(e, slot=cslot, m=m, r=r, half=half):
                        s2 = slot[:, :].rearrange("p (t d) -> p t d", d=128)
                        s2 = s2[:, half * 8:(half + 1) * 8, :]
                        dst = wsc[MIDX[m], half * 8:(half + 1) * 8, :, r, :]
                        return e.dma_start(out=dst.transpose([1, 0, 2]), in_=s2)
                    op("gpsimd", bf, [(s_sm[(m, sl)], 16)])
                cast_rel[sl] = (s_sm[(m, sl)], s_sm[(m, sl)].n)
            return step

        def mk_step_scale():
            def step():
                wait("vector", s_dv, s_dv.n)
                op("vector", lambda e: e.reduce_sum(
                    scr[:, 0:1], stats[:, 0:NT], axis=AX), [(s_dv, 1)])
                wait("vector", s_dv, s_dv.n)
                wait("tensor", s_dv, s_dv.n)
                r0 = psA_bank_rel[0]
                if r0 is not None:
                    wait("tensor", r0[0], r0[1])
                op("tensor", lambda e: e.matmul(
                    psA[0:1, 0:1], scr[:, 0:1], onescol_f[:, :],
                    start=True, stop=True), [(s_pe, 1)])
                wait("vector", s_pe, s_pe.n)
                vtiny(lambda e: e.tensor_scalar(
                    sc_sc[:, :], psA[0:1, 0:1], 1.0 / (D * D), None, ALU.mult))
                wait("tensor", s_dv, s_dv.n)
                op("tensor", lambda e: e.matmul(
                    psA[:, 1:2], onesrow_f[:, :], sc_sc[:, :],
                    start=True, stop=True), [(s_pe, 1)])
                wait("vector", s_pe, s_pe.n)
                vtiny(lambda e: e.tensor_copy(scr[:, 1:2], psA[:, 1:2]))
                vtiny(lambda e: e.reciprocal(rscale[:, :], scr[:, 1:2]))
                psA_bank_rel[0] = (s_dv, s_dv.n)
            return step

        for m in ("q", "k", "v", "o"):
            for r in range(NT):
                pro_steps.append(mk_step_convert(m, r, "cast"))
        for r in range(NT):
            pro_steps.append(mk_step_convert("f", r, "absred"))
        pro_steps.append(mk_step_scale())
        for r in range(NT):
            pro_steps.append(mk_step_convert("f", r, "tern"))
        PRO_READY = {"q": 16, "k": 32, "v": 48, "o": 64, "f": len(pro_steps)}
        pumped = [0]

        def pump(n):
            while n > 0 and pumped[0] < len(pro_steps):
                pro_steps[pumped[0]]()
                pumped[0] += 1
                n -= 1

        def pump_until(idx):
            pump(max(0, idx - pumped[0]))

        wtile_ctr = [0]
        pending_wfree = [0]

        def stream_whalf(m, to, half):
            k = wtile_ctr[0]
            wtile_ctr[0] += 1
            if k >= NRING:
                wait("sync", s_wfree, k - (NRING - 1))
            if to == 0 and half == 0:
                pump_until(PRO_READY[m])
                wait("sync", s_sm[(m, 0)], s_sm[(m, 0)].n)
                wait("sync", s_sm[(m, 1)], s_sm[(m, 1)].n)
            op("sync", lambda e, m=m, to=to, k=k, half=half: e.dma_start(
                out=wring[k % NRING][:, :].rearrange("p (t d) -> p t d", d=128),
                in_=wsc[MIDX[m], to, :, half * 8:(half + 1) * 8, :]),
               [(s_w[k % NRING], 16)])
            return k, (s_w[k % NRING], s_w[k % NRING].n)

        def load_xln(c):
            sl = c % 4
            wait_rel("gpsimd", f"xln{sl}")
            op("gpsimd", lambda e, c=c, sl=sl: e.dma_start(
                out=xln[sl][:, :], in_=x_in[c * 128:(c + 1) * 128, :]),
               [(s_gx[sl], 16)])
            return (s_gx[sl], s_gx[sl].n)

        def load_xres(c):
            sl = c % 2
            wait_rel("gpsimd", f"xres{sl}")
            op("gpsimd", lambda e, c=c, sl=sl: e.dma_start(
                out=xres[sl][:, :], in_=x_in[c * 128:(c + 1) * 128, :]),
               [(s_xr[sl], 16)])
            return (s_xr[sl], s_xr[sl].n)

        ac_kvbf = 0
        dv_kvbf_state = [0]
        xres_handover = [False]

        pend_ln = {}

        def emit_ln_A(sc2, gate_dv):
            L_SUM, L_SQ, L_MEAN, L_M2, L_VAR, L_U, L_RU, L_W2 = 0, 4, 8, 12, 16, 20, 24, 28
            L_LNU, L_R1, L_LNW2, L_R2, L_NMR, L_SC2, L_NMR2 = 32, 36, 40, 44, 48, 52, 56
            loaded = {}
            for j in range(NCH):
                loaded[j] = load_xln(sc2 * NCH + j)
            ac_sq = 0
            for j in range(NCH):
                xs = xln[(sc2 * NCH + j) % 4]
                wait("scalar", loaded[j][0], loaded[j][1])
                if j == 0 and gate_dv:
                    wait("scalar", s_dv, gate_dv)
                if j > 0:
                    wait("scalar", s_ac, ac_sq)
                op("scalar", lambda e, xs=xs, j=j: e.activation(
                    scr[:, RZ[0]:RZ[1]], xs[:, :], AF.Square,
                    accum_out=lst[:, L_SQ + j:L_SQ + j + 1]), [(s_ac, 1)])
                ac_sq = s_ac.n
                wait("vector", loaded[j][0], loaded[j][1])
                op("vector", lambda e, xs=xs, j=j: e.reduce_sum(
                    lst[:, L_SUM + j:L_SUM + j + 1], xs[:, :], axis=AX),
                   [(s_dv, 1)])
            wait("vector", s_dv, s_dv.n)
            c4 = lambda base: lst[:, base:base + 4]
            vtiny(lambda e: e.tensor_scalar(
                c4(L_MEAN), c4(L_SUM), 1.0 / D, None, ALU.mult))
            vtiny(lambda e: e.tensor_tensor(
                c4(L_M2), c4(L_MEAN), c4(L_MEAN), ALU.mult))
            wait("vector", s_ac, ac_sq)
            vtiny(lambda e: e.scalar_tensor_tensor(
                c4(L_VAR), c4(L_SQ), 1.0 / D, c4(L_M2), ALU.mult, ALU.subtract))
            vtiny(lambda e: e.tensor_scalar(
                c4(L_U), c4(L_VAR), EPS_LN, None, ALU.add))
            vtiny(lambda e: e.reciprocal(c4(L_RU), c4(L_U)))
            vtiny(lambda e: e.tensor_tensor(
                c4(L_W2), c4(L_VAR), c4(L_RU), ALU.mult))
            vtiny(lambda e: e.tensor_scalar(
                c4(L_W2), c4(L_W2), EPS_LN, None, ALU.add))
            wait("scalar", s_dv, s_dv.n)
            atiny(lambda e: e.activation(c4(L_LNU), c4(L_U), AF.Ln))
            atiny(lambda e: e.activation(c4(L_R1), c4(L_LNU), AF.Exp, scale=-0.5))
            atiny(lambda e: e.activation(c4(L_LNW2), c4(L_W2), AF.Ln))
            atiny(lambda e: e.activation(c4(L_R2), c4(L_LNW2), AF.Exp, scale=-0.5))
            wait("vector", s_ac, s_ac.n)
            vtiny(lambda e: e.scalar_tensor_tensor(
                c4(L_NMR), c4(L_MEAN), -1.0, c4(L_R1), ALU.mult, ALU.mult))
            vtiny(lambda e: e.tensor_tensor(
                c4(L_SC2), c4(L_R1), c4(L_R2), ALU.mult))
            vtiny(lambda e: e.scalar_tensor_tensor(
                c4(L_NMR2), c4(L_MEAN), -1.0, c4(L_SC2), ALU.mult, ALU.mult))
            dv_aff = s_dv.n
            wait("scalar", s_dv, dv_aff)
            ac_applies = {}
            for j in range(2):
                c = sc2 * NCH + j
                xs = xln[c % 4]
                xntm = [kftok, vtok][j % 2]
                xnstm = xns_tm[j % 2]
                wait_rel("scalar", "kftok" if j % 2 == 0 else "vtok")
                op("scalar", lambda e, xs=xs, xm=xntm, j=j: e.activation(
                    xm[:, :], xs[:, :], AF.Identity,
                    bias=lst[:, L_NMR + j:L_NMR + j + 1],
                    scale=lst[:, L_R1 + j:L_R1 + j + 1]), [(s_ac, 1)])
                wait_rel("scalar", f"xns{j % 2}")
                op("scalar", lambda e, xs=xs, xm=xnstm, j=j: e.activation(
                    xm[:, :], xs[:, :], AF.Identity,
                    bias=lst[:, L_NMR2 + j:L_NMR2 + j + 1],
                    scale=lst[:, L_SC2 + j:L_SC2 + j + 1]), [(s_ac, 1)])
                ac_applies[j] = s_ac.n
                set_rel(f"xln{c % 4}", (s_ac, s_ac.n))
            pend_ln[sc2] = (ac_applies, s_dv.n)

        def emit_apply(sc2, j, ac_applies):
            L_R1, L_NMR, L_SC2, L_NMR2 = 36, 48, 52, 56
            c = sc2 * NCH + j
            xs = xln[c % 4]
            xntm = [kftok, vtok][j % 2]
            xnstm = xns_tm[j % 2]
            wait_rel("scalar", "kftok" if j % 2 == 0 else "vtok")
            op("scalar", lambda e, xs=xs, xm=xntm, j=j: e.activation(
                xm[:, :], xs[:, :], AF.Identity,
                bias=lst[:, L_NMR + j:L_NMR + j + 1],
                scale=lst[:, L_R1 + j:L_R1 + j + 1]), [(s_ac, 1)])
            wait_rel("scalar", f"xns{j % 2}")
            op("scalar", lambda e, xs=xs, xm=xnstm, j=j: e.activation(
                xm[:, :], xs[:, :], AF.Identity,
                bias=lst[:, L_NMR2 + j:L_NMR2 + j + 1],
                scale=lst[:, L_SC2 + j:L_SC2 + j + 1]), [(s_ac, 1)])
            ac_applies[j] = s_ac.n
            set_rel(f"xln{c % 4}", (s_ac, s_ac.n))

        def emit_ln_B(sc2):
            ac_applies, dv_aff = pend_ln.pop(sc2)
            for j in range(NCH):
                xntm = [kftok, vtok][j % 2]
                xnstm = xns_tm[j % 2]
                wait("tensor", s_ac, ac_applies[j])
                wait("tensor", s_ini, init_gdma)
                wait_rel("tensor", "psB")
                for t in range(NT):
                    incs = [(s_pe, 1)] if t == NT - 1 else []
                    op("tensor", lambda e, t=t, xm=xnstm: e.transpose(
                        psBT[:, 2048 + t * 128:2048 + (t + 1) * 128],
                        xm[:, t * 128:(t + 1) * 128], identb[:, :]), incs)
                pe_tr2 = s_pe.n
                for t in range(NT):
                    incs = [(s_pe, 1)] if t == NT - 1 else []
                    op("tensor", lambda e, t=t, xm=xntm: e.transpose(
                        psBT[:, t * 128:(t + 1) * 128],
                        xm[:, t * 128:(t + 1) * 128], identb[:, :]), incs)
                pe_tr1 = s_pe.n
                set_rel("kftok" if j % 2 == 0 else "vtok", (s_pe, pe_tr1))
                set_rel(f"xns{j % 2}", (s_pe, pe_tr2))
                if j + 2 < NCH:
                    emit_apply(sc2, j + 2, ac_applies)
                wait("vector", s_pe, pe_tr1)
                wait_rel("vector", "xnT")
                op("vector", lambda e, j=j: e.tensor_copy(
                    bass.AP(xnT, j * 128, [[NT * T, 128], [T, NT], [1, 128]]),
                    psBT_lo3d()), [(s_dv, 1)])
                dv_xnT = s_dv.n
                wait("vector", s_pe, pe_tr2)
                wait("scalar", s_pe, pe_tr2)
                wait_rel("vector", "xn2T")
                for t in range(0, NT, 2):
                    incs = [(s_dv, 1)] if t == NT - 2 else []
                    op("vector", lambda e, t=t, j=j: e.tensor_scalar(
                        xn2T[:, t * T + j * 128: t * T + (j + 1) * 128],
                        psBT[:, 2048 + t * 128:2048 + (t + 1) * 128],
                        lnw_sb[:, t:t + 1], lnb_sb[:, t:t + 1],
                        ALU.mult, ALU.add), incs)
                for t in range(1, NT, 2):
                    incs = [(s_ac, 1)] if t == NT - 1 else []
                    op("scalar", lambda e, t=t, j=j: e.activation(
                        xn2T[:, t * T + j * 128: t * T + (j + 1) * 128],
                        psBT[:, 2048 + t * 128:2048 + (t + 1) * 128],
                        AF.Identity, bias=lnb_sb[:, t:t + 1],
                        scale=lnw_sb[:, t:t + 1]), incs)
                set_rel("psB", (s_dv, s_dv.n), (s_ac, s_ac.n))
                pump(2)
            return s_dv.n, s_ac.n

        for sc in range(nsc):
            if sc == 0:
                emit_ln_A(0, 0)
            dv_ln_done, ac_ln_done = emit_ln_B(sc)

            def proj_tile(m, to, evac_fn, rhs_buf, rhs_deps, ps=None):
                ps_t = psA if ps is None else ps
                kA, wnA = stream_whalf(m, to, 0)
                kB, wnB = stream_whalf(m, to, 1)
                bk = to % 4
                if ps is None:
                    r0 = psA_bank_rel[bk]
                    if r0 is not None:
                        wait("tensor", r0[0], r0[1])
                if to == 0:
                    for sem, cnt in rhs_deps:
                        wait("tensor", sem, cnt)
                wait("tensor", wnA[0], wnA[1])
                for ti in range(NT):
                    k = kA if ti < 8 else kB
                    incs = []
                    if ti == 0 and pending_wfree[0]:
                        incs = [(s_wfree, pending_wfree[0])]
                        pending_wfree[0] = 0
                    if ti == 8:
                        incs = [(s_wfree, 1)]
                        wait("tensor", wnB[0], wnB[1])
                    if ti == NT - 1:
                        incs = [(s_pe, 1)]
                    op("tensor", lambda e, k=k, ti=ti, bk=bk, rb=rhs_buf, pt=ps_t:
                       e.matmul(
                           pt[:, bk * 512:(bk + 1) * 512],
                           wring[k % NRING][:, (ti % 8) * 128:(ti % 8 + 1) * 128],
                           rb[:, ti * T:(ti + 1) * T],
                           start=(ti == 0), stop=(ti == NT - 1)), incs)
                pending_wfree[0] += 1
                evac_fn(to, bk, s_pe.n)
                pump(2)

            def proj_phase(m, evac_fn, rhs_buf, rhs_deps):
                for to in range(NT):
                    proj_tile(m, to, evac_fn, rhs_buf, rhs_deps)

            def mk_evac_qk(dst, bias_key):
                def evac(to, bk, pe_n):
                    esl = EL_E[to % 2]
                    rsl = EL_R[to % 2]
                    wait("scalar", s_pe, pe_n)
                    wait_rel("scalar", f"el{to % 2}")
                    op("scalar", lambda e, bk=bk, to=to, esl=esl: e.activation(
                        scr[:, esl[0]:esl[1]], psA[:, bk * 512:(bk + 1) * 512],
                        AF.Exp, bias=biases[bias_key][:, to:to + 1]), [(s_ac, 1)])
                    op("scalar", lambda e, bk=bk, to=to, rsl=rsl: e.activation(
                        scr[:, rsl[0]:rsl[1]], psA[:, bk * 512:(bk + 1) * 512],
                        AF.Relu, bias=biases[bias_key][:, to:to + 1]), [(s_ac, 1)])
                    psA_bank_rel[bk] = (s_ac, s_ac.n)
                    wait("vector", s_ac, s_ac.n)
                    op("vector", lambda e, to=to, esl=esl, rsl=rsl:
                       e.scalar_tensor_tensor(
                           dst[:, to * T:(to + 1) * T], scr[:, esl[0]:esl[1]], 1.0,
                           scr[:, rsl[0]:rsl[1]], ALU.min, ALU.add), [(s_dv, 1)])
                    set_rel(f"el{to % 2}", (s_dv, s_dv.n))
                return evac

            proj_phase("q", mk_evac_qk(qat, "q"), xnT, [(s_dv, dv_ln_done)])
            q_done_dv = s_dv.n
            proj_phase("k", mk_evac_qk(kfT, "k"), xnT, [(s_dv, dv_ln_done)])
            k_done_dv = s_dv.n
            wait("vector", s_dv, s_dv.n)
            for j in range(NCH):
                op("vector", lambda e, j=j: e.reduce_sum(
                    bass.AP(kscur, j * H, [[4 * H, 128], [1, H], [0, 1]]),
                    bass.AP(kfT, j * 128, [[NT * T, 128], [T, H], [1, 128]]),
                    axis=AX), [(s_dv, 1)])
            dv_ksum4 = s_dv.n

            def evac_v(to, bk, pe_n):
                wait("scalar", s_pe, pe_n)
                op("scalar", lambda e, to=to, bk=bk: e.activation(
                    vg[:, to * T:(to + 1) * T], psA[:, bk * 512:(bk + 1) * 512],
                    AF.Identity, bias=biases["v"][:, to:to + 1],
                    scale=1.0), [(s_ac, 1)])
                psA_bank_rel[bk] = (s_ac, s_ac.n)

            proj_phase("v", evac_v, xnT, [(s_dv, dv_ln_done)])
            v_done_ac = s_ac.n

            def evac_f(to, bk, pe_n):
                wait("scalar", s_pe, pe_n)
                op("scalar", lambda e, to=to, bk=bk: e.activation(
                    xnT[:, to * T:(to + 1) * T], psB[:, bk * 512:(bk + 1) * 512],
                    AF.Silu), [(s_ac, 1)])
                add_rel("psB", (s_ac, s_ac.n))

            for j in range(NCH):
                wait_rel("tensor", "psB")
                wait("tensor", s_dv, k_done_dv)
                wait("tensor", s_ac, v_done_ac)
                for h in range(H):
                    incs = [(s_pe, 1)] if h == H - 1 else []
                    op("tensor", lambda e, h=h, j=j: e.transpose(
                        psBT[:, h * 128:(h + 1) * 128],
                        kfT[:, h * T + j * 128: h * T + (j + 1) * 128],
                        identb[:, :]), incs)
                pe_kft = s_pe.n
                for h in range(H):
                    incs = [(s_pe, 1)] if h == H - 1 else []
                    op("tensor", lambda e, h=h, j=j: e.transpose(
                        psBT[:, 2048 + h * 128:2048 + (h + 1) * 128],
                        vg[:, h * T + j * 128: h * T + (j + 1) * 128],
                        identb[:, :]), incs)
                pe_vt = s_pe.n
                wait("scalar", s_pe, pe_kft)
                wait_rel("scalar", "kftok")
                op("scalar", lambda e: e.copy(
                    kftok[:, :], psBT[:, 0:2048]), [(s_ac, 1)])
                ac_kftok = s_ac.n
                wait("scalar", s_pe, pe_vt)
                wait_rel("scalar", "vtok")
                op("scalar", lambda e: e.copy(
                    vtok[:, :], psBT[:, 2048:4096]), [(s_ac, 1)])
                ac_vtok = s_ac.n
                wait("tensor", s_dv, q_done_dv)
                for bk in range(4):
                    r0 = psA_bank_rel[bk]
                    if r0 is not None:
                        wait("tensor", r0[0], r0[1])
                for h in range(H):
                    incs = [(s_pe, 1)] if h == H - 1 else []
                    op("tensor", lambda e, h=h, j=j: e.matmul(
                        psA[:, h * 128:(h + 1) * 128],
                        kfT[:, h * T + j * 128: h * T + (j + 1) * 128],
                        qat[:, h * T + j * 128: h * T + (j + 1) * 128],
                        start=True, stop=True), incs)
                pe_a = s_pe.n
                wait("vector", s_pe, pe_a)
                wait_rel("vector", "a_m")
                op("vector", lambda e: e.tensor_tensor(
                    a_m[:, 0:1024].rearrange("p (t c) -> p t c", c=128),
                    psA[:, 0:1024].rearrange("p (t c) -> p t c", c=128),
                    maskb[:, :].unsqueeze(1).broadcast_to([128, H // 2, 128]),
                    ALU.mult), [(s_dv, 1)])
                dv_am_a = s_dv.n
                op("vector", lambda e: e.tensor_tensor(
                    a_m[:, 1024:2048].rearrange("p (t c) -> p t c", c=128),
                    psA[:, 1024:2048].rearrange("p (t c) -> p t c", c=128),
                    maskb[:, :].unsqueeze(1).broadcast_to([128, H // 2, 128]),
                    ALU.mult), [(s_dv, 1)])
                dv_am_b = s_dv.n
                wait("tensor", s_dv, dv_kvbf_state[0])
                for h in range(H):
                    if h == 0:
                        wait("tensor", s_dv, dv_am_a)
                    if h == H // 2:
                        wait("tensor", s_dv, dv_am_b)
                    op("tensor", lambda e, h=h, j=j: e.matmul(
                        psA[:, h * 128:(h + 1) * 128],
                        bass.AP(ksumbf, h, [[H, 128], [0, 128]]),
                        qat[:, h * T + j * 128: h * T + (j + 1) * 128],
                        start=True, stop=False))
                    incs = [(s_pe, 1)] if h == H - 1 else []
                    op("tensor", lambda e, h=h: e.matmul(
                        psA[:, h * 128:(h + 1) * 128],
                        bass.AP(onescol_b, 0, [[1, 128], [0, 128]]),
                        a_m[:, h * 128:(h + 1) * 128],
                        start=False, stop=True), incs)
                pe_z = s_pe.n
                wait("tensor", s_ac, ac_vtok)
                wait("tensor", s_ac, ac_kftok)
                for h in range(H):
                    op("tensor", lambda e, h=h, j=j: e.matmul(
                        psB[:, h * 128:(h + 1) * 128],
                        kvbf[:, h * 128:(h + 1) * 128],
                        qat[:, h * T + j * 128: h * T + (j + 1) * 128],
                        start=True, stop=False))
                    incs = [(s_pe, 1)] if h == H - 1 else []
                    op("tensor", lambda e, h=h: e.matmul(
                        psB[:, h * 128:(h + 1) * 128],
                        vtok[:, h * 128:(h + 1) * 128],
                        a_m[:, h * 128:(h + 1) * 128],
                        start=False, stop=True), incs)
                pe_io = s_pe.n
                wait("scalar", s_pe, pe_z)
                op("scalar", lambda e: e.activation(
                    scr[:, SQ_DUMP[0]:SQ_DUMP[1]], psA[:, :], AF.Ln,
                    bias=epscol_f[:, :]), [(s_ac, 1)])
                ac_lnz = s_ac.n
                wait("scalar", s_ac, ac_lnz)
                op("scalar", lambda e: e.activation(
                    scr[:, RZ[0]:RZ[1]], scr[:, SQ_DUMP[0]:SQ_DUMP[1]],
                    AF.Exp, scale=-1.0), [(s_ac, 1)])
                ac_rz = s_ac.n
                wait("tensor", s_ac, ac_lnz)
                for h in range(H):
                    incs = [(s_pe, 1)] if h == H - 1 else []
                    op("tensor", lambda e, h=h: e.matmul(
                        psA[:, h * 128:(h + 1) * 128],
                        kftok[:, h * 128:(h + 1) * 128],
                        vtok[:, h * 128:(h + 1) * 128],
                        start=True, stop=True), incs)
                pe_kv = s_pe.n
                wait("vector", s_pe, pe_io)
                wait("vector", s_ac, ac_rz)
                op("vector", lambda e, j=j: e.tensor_tensor(
                    bass.AP(qat, j * 128, [[NT * T, 128], [T, H], [1, 128]]),
                    psB[:, :].rearrange("p (t c) -> p t c", c=128),
                    scr[:, RZ[0]:RZ[1]].rearrange("p (t c) -> p t c", c=128),
                    ALU.mult), [(s_dv, 1)])
                set_rel("psB", (s_dv, s_dv.n), (s_ac, ac_vtok))
                wait("vector", s_pe, pe_kv)
                op("vector", lambda e: e.tensor_tensor(
                    kvbf[:, :], kvstate[:, :], psA[:, :], ALU.add), [(s_dv, 1)])
                dv_kvbf = s_dv.n
                wait("vector", s_dv, dv_ksum4)
                op("vector", lambda e, j=j: e.tensor_tensor(
                    ksumbf[:, :], ksumst[:, :],
                    kscur[:, j * H:(j + 1) * H], ALU.add), [(s_dv, 1)])
                dv_ksbf = s_dv.n
                op("vector", lambda e: e.tensor_tensor(
                    kvstate[:, :], kvstate[:, :], psA[:, :], ALU.add), [(s_dv, 1)])
                dv_kvupd = s_dv.n
                for bk in range(4):
                    psA_bank_rel[bk] = (s_dv, dv_kvupd)
                wait("vector", s_dv, dv_ksbf)
                vtiny(lambda e, j=j: e.tensor_tensor(
                    ksumst[:, :], ksumst[:, :],
                    kscur[:, j * H:(j + 1) * H], ALU.add))
                dv_kvbf_state[0] = dv_kvbf
                set_rel("kftok", (s_pe, pe_kv))
                set_rel("vtok", (s_pe, pe_kv))
                set_rel("a_m", (s_pe, pe_io))
                wait_rel("tensor", "psB")
                set_rel("psB")
                for to in range(4 * j, 4 * j + 4):
                    proj_tile("f", to, evac_f, xn2T,
                              [(s_ac, ac_ln_done)], ps=psB)
            f_done_ac = s_ac.n
            attn_dv_done = s_dv.n
            set_rel("el0", (s_dv, attn_dv_done))
            set_rel("el1", (s_dv, attn_dv_done))

            if sc + 1 < nsc:
                emit_ln_A(sc + 1, attn_dv_done)

            if not xres_handover[0]:
                pump_until(len(pro_steps))
                for sl2 in range(2):
                    if loadslot_rel[sl2] is not None:
                        set_rel(f"xres{sl2}", loadslot_rel[sl2])
                xres_handover[0] = True
            xres_loaded = {}
            xres_loaded[0] = load_xres(sc * NCH + 0)
            xres_loaded[1] = load_xres(sc * NCH + 1)

            dv_sg = {}

            def evac_o(to, bk, pe_n):
                ssl = STMP[to % 2]
                wait("scalar", s_pe, pe_n)
                wait_rel("scalar", f"st{to % 2}")
                op("scalar", lambda e, to=to, bk=bk, ssl=ssl: e.activation(
                    scr[:, ssl[0]:ssl[1]], psA[:, bk * 512:(bk + 1) * 512],
                    AF.Identity, bias=biases["o"][:, to:to + 1], scale=1.0,
                    ), [(s_ac, 1)])
                psA_bank_rel[bk] = (s_ac, s_ac.n)
                wait("vector", s_ac, s_ac.n)
                if to == 0:
                    wait("vector", s_ac, f_done_ac)
                op("vector", lambda e, to=to, ssl=ssl: e.tensor_tensor(
                    vg[:, to * T:(to + 1) * T], scr[:, ssl[0]:ssl[1]],
                    xnT[:, to * T:(to + 1) * T], ALU.mult), [(s_dv, 1)])
                dv_sg[to] = s_dv.n
                set_rel(f"st{to % 2}", (s_dv, s_dv.n))

            proj_phase("o", evac_o, qat, [(s_dv, attn_dv_done)])
            sg_dv_done = s_dv.n

            for j in range(NCH):
                c = sc * NCH + j
                wait_rel("tensor", "psB")
                for t in range(NT):
                    wait("tensor", s_dv, dv_sg[t])
                    incs = [(s_pe, 1)] if t == NT - 1 else []
                    op("tensor", lambda e, t=t, j=j: e.transpose(
                        psBT[:, t * 128:(t + 1) * 128],
                        vg[:, t * T + j * 128: t * T + (j + 1) * 128],
                        identb[:, :]), incs)
                pe_sg = s_pe.n
                sl = c % 2
                wait("vector", s_pe, pe_sg)
                wait("vector", xres_loaded[j][0], xres_loaded[j][1])
                op("vector", lambda e, sl=sl: e.tensor_tensor(
                    xres[sl][:, :].rearrange("p (t d) -> p t d", d=128),
                    psBT_lo3d(),
                    xres[sl][:, :].rearrange("p (t d) -> p t d", d=128),
                    ALU.add), [(s_dv, 1)])
                dv_out = s_dv.n
                set_rel("psB", (s_dv, dv_out))
                wait("gpsimd", s_dv, dv_out)
                op("gpsimd", lambda e, c=c, sl=sl: e.dma_start(
                    out=y_out[c * 128:(c + 1) * 128, :], in_=xres[sl][:, :]),
                   [(s_xr[sl], 16)])
                set_rel(f"xres{sl}", (s_xr[sl], s_xr[sl].n))
                if j + 2 < NCH:
                    xres_loaded[j + 2] = load_xres(sc * NCH + j + 2)
            set_rel("xnT", (s_dv, sg_dv_done))
            set_rel("vg_next", (s_pe, s_pe.n))

        wait("gpsimd", s_xr[0], s_xr[0].n)
        wait("gpsimd", s_xr[1], s_xr[1].n)

        if debug:
            dbg_specs = [
                ("xnT", xnT, CDT), ("xn2T", xn2T, CDT), ("qat", qat, CDT),
                ("kfT", kfT, CDT), ("vg", vg, CDT), ("kvstate", kvstate, F32),
                ("stats", stats, F32), ("rscale", rscale, F32),
                ("kftok", kftok, CDT), ("vtok", vtok, CDT), ("a_m", a_m, CDT),
                ("ksumst", ksumst, F32), ("scr", scr, F32),
                ("kvbf", kvbf, CDT),
            ]
            wait("gpsimd", s_dv, s_dv.n)
            wait("gpsimd", s_ac, s_ac.n)
            wait("gpsimd", s_pe, s_pe.n)
            for name, buf, dt in dbg_specs:
                dbg_out = nc.dram_tensor("dbg_" + name, list(buf.shape), dt,
                                         kind="ExternalOutput")
                op("gpsimd", lambda e, o=dbg_out, b=buf: e.dma_start(
                    out=o[:, :], in_=b[:, :]), [(s_ini, 16)])
            wait("gpsimd", s_ini, s_ini.n)

        block = e_(nc.Block())

        @block.sync
        def _(sync):
            for fn in prog["sync"]:
                fn(sync)

        @block.gpsimd
        def _(gpsimd):
            for fn in prog["gpsimd"]:
                fn(gpsimd)

        @block.scalar
        def _(scalar):
            for fn in prog["scalar"]:
                fn(scalar)

        @block.vector
        def _(vector):
            for fn in prog["vector"]:
                fn(vector)

        @block.tensor
        def _(tensor):
            for fn in prog["tensor"]:
                fn(tensor)

    return nc


_NC_CACHE = {}


def _get_nc(nsc):
    if nsc not in _NC_CACHE:
        _NC_CACHE[nsc] = build_nc(nsc)
    return _NC_CACHE[nsc]


def kernel(L0_attn_x, L0_attn_Wq, L0_attn_bq, L0_attn_Wk, L0_attn_bk,
           L0_attn_Wv, L0_attn_bv, L0_attn_Wo, L0_attn_bo,
           L0_ffn_W, L0_ffn_ln_w, L0_ffn_ln_b):
    nc = _get_nc(S // T)
    x = np.ascontiguousarray(np.asarray(L0_attn_x, dtype=np.float32))

    def colmajor16(v):
        return np.ascontiguousarray(
            np.asarray(v, dtype=np.float32).reshape(NT, 128).T)

    common = {
        "w_q": np.ascontiguousarray(np.asarray(L0_attn_Wq, dtype=np.float32)),
        "w_k": np.ascontiguousarray(np.asarray(L0_attn_Wk, dtype=np.float32)),
        "w_v": np.ascontiguousarray(np.asarray(L0_attn_Wv, dtype=np.float32)),
        "w_o": np.ascontiguousarray(np.asarray(L0_attn_Wo, dtype=np.float32)),
        "w_f": np.ascontiguousarray(np.asarray(L0_ffn_W, dtype=np.float32)),
        "b_q": colmajor16(L0_attn_bq),
        "b_k": colmajor16(L0_attn_bk),
        "b_v": colmajor16(L0_attn_bv),
        "b_o": colmajor16(L0_attn_bo),
        "lnw": colmajor16(L0_ffn_ln_w),
        "lnb": colmajor16(L0_ffn_ln_b),
        "identb": np.eye(128, dtype=ml_dtypes.bfloat16),
        "maskb": np.triu(np.ones((128, 128), dtype=ml_dtypes.bfloat16)),
    }
    in_maps = []
    for b in range(N_CORES):
        m = dict(common)
        m["x"] = np.ascontiguousarray(x[b])
        in_maps.append(m)
    res = run_bass_kernel_spmd(nc, in_maps, core_ids=list(range(N_CORES)))
    out = np.stack([res.results[b]["y"] for b in range(N_CORES)], axis=0)
    return out


# version 15
# speedup vs baseline: 1.0945x; 1.0358x over previous
import numpy as np
import ml_dtypes
from contextlib import ExitStack

import concourse.bass as bass
import concourse.mybir as mybir
from concourse.bass_utils import run_bass_kernel_spmd

F32 = mybir.dt.float32
BF16 = mybir.dt.bfloat16
CDT = BF16
AF = mybir.ActivationFunctionType
ALU = mybir.AluOpType
AX = mybir.AxisListType.X

S = 4096
D = 2048
H = 16
C = 128
T = 512
NCH = T // C
NT = D // 128
EPS_LN = 1e-5
EPS_AT = 1e-6
N_CORES = 8


class Sem:
    def __init__(self, name):
        self.name = name
        self.n = 0
        self.h = None


def build_nc(nsc=S // T, debug=False):
    nc = bass.Bass("TRN2", target_bir_lowering=False)

    x_in = nc.dram_tensor("x", [S, D], F32, kind="ExternalInput")
    y_out = nc.dram_tensor("y", [S, D], F32, kind="ExternalOutput")
    w_in = {m: nc.dram_tensor("w_" + m, [D, D], F32, kind="ExternalInput")
            for m in ("q", "k", "v", "o", "f")}
    b_in = {m: nc.dram_tensor("b_" + m, [128, NT], F32, kind="ExternalInput")
            for m in ("q", "k", "v", "o")}
    lnw_in = nc.dram_tensor("lnw", [128, NT], F32, kind="ExternalInput")
    lnb_in = nc.dram_tensor("lnb", [128, NT], F32, kind="ExternalInput")
    identb_in = nc.dram_tensor("identb", [128, 128], BF16, kind="ExternalInput")
    maskb_in = nc.dram_tensor("maskb", [128, 128], BF16, kind="ExternalInput")
    wsc = nc.dram_tensor("wsc", [5, NT, 128, NT, 128], CDT, kind="Internal")
    MIDX = {"q": 0, "k": 1, "v": 2, "o": 3, "f": 4}

    prog = {k: [] for k in ("sync", "gpsimd", "scalar", "vector", "tensor")}
    s_w = [Sem(f"w{i}") for i in range(10)]
    s_xl = [Sem(f"xl{i}") for i in range(2)]
    s_gx = [Sem(f"gx{i}") for i in range(4)]
    s_xr = [Sem(f"xr{i}") for i in range(2)]
    s_sm = {(m, sl): Sem(f"st_{m}{sl}")
            for m in ("q", "k", "v", "o", "f") for sl in range(2)}
    s_ini = Sem("ini")
    s_wfree = Sem("wfree")
    s_pe = Sem("pe")
    s_dv = Sem("dv")
    s_ac = Sem("ac")
    all_sems = (s_w + s_xl + s_gx + s_xr + list(s_sm.values())
                + [s_ini, s_wfree, s_pe, s_dv, s_ac])

    def wait(eng, sem, val):
        if sem is None or val is None or val <= 0:
            return
        prog[eng].append(lambda e, s=sem, v=val: e.wait_ge(s.h, v))

    def op(eng, build_fn, incs=()):
        vals = []
        for s, k in incs:
            s.n += k
            vals.append((s, k))

        def run(e, bf=build_fn, vals=tuple(vals)):
            i = bf(e)
            for s, k in vals:
                i = i.then_inc(s.h, k)
        prog[eng].append(run)

    rel = {}

    def set_rel(key, *deps):
        rel[key] = [d for d in deps if d is not None]

    def add_rel(key, dep):
        rel.setdefault(key, []).append(dep)

    def wait_rel(eng, *keys):
        for key in keys:
            for sem, cnt in rel.get(key, ()):
                wait(eng, sem, cnt)

    def vtiny(build_fn):
        op("vector", build_fn, [(s_dv, 1)])
        wait("vector", s_dv, s_dv.n)

    def atiny(build_fn):
        op("scalar", build_fn, [(s_ac, 1)])
        wait("scalar", s_ac, s_ac.n)

    with ExitStack() as ctx:
        e_ = ctx.enter_context

        NRING = 10
        wring = [e_(nc.sbuf_tensor(f"wring{i}", [128, D // 2], CDT)) for i in range(NRING)]
        xln = [e_(nc.sbuf_tensor(f"xln{i}", [128, D], F32)) for i in range(4)]
        xres = [e_(nc.sbuf_tensor(f"xres{i}", [128, D], F32)) for i in range(2)]
        xnT = e_(nc.sbuf_tensor("xnT", [128, NT * T], CDT))
        xn2T = e_(nc.sbuf_tensor("xn2T", [128, NT * T], CDT))
        qat = e_(nc.sbuf_tensor("qat", [128, NT * T], CDT))
        kfT = e_(nc.sbuf_tensor("kfT", [128, NT * T], CDT))
        vg = e_(nc.sbuf_tensor("vg", [128, NT * T], CDT))
        kvstate = e_(nc.sbuf_tensor("kvstate", [128, D], F32))
        kvbf = e_(nc.sbuf_tensor("kvbf", [128, D], CDT))
        ksumst = e_(nc.sbuf_tensor("ksumst", [128, H], F32))
        ksumbf = e_(nc.sbuf_tensor("ksumbf", [128, H], CDT))
        kscur = e_(nc.sbuf_tensor("kscur", [128, 4 * H], F32))
        kftok = e_(nc.sbuf_tensor("kftok", [128, D], CDT))
        vtok = e_(nc.sbuf_tensor("vtok", [128, D], CDT))
        xns_tm = [e_(nc.sbuf_tensor(f"xns{i}", [128, D], CDT)) for i in range(2)]
        a_m = e_(nc.sbuf_tensor("a_m", [128, D], CDT))
        scr = e_(nc.sbuf_tensor("scr", [128, 4096], F32))
        stats = e_(nc.sbuf_tensor("stats", [128, 16], F32))
        lst = e_(nc.sbuf_tensor("lst", [128, 64], F32))
        biases = {m: e_(nc.sbuf_tensor("bs_" + m, [128, NT], F32))
                  for m in ("q", "k", "v", "o")}
        lnw_sb = e_(nc.sbuf_tensor("lnw_sb", [128, NT], F32))
        lnb_sb = e_(nc.sbuf_tensor("lnb_sb", [128, NT], F32))
        identb = e_(nc.sbuf_tensor("identb_s", [128, 128], BF16))
        maskb = e_(nc.sbuf_tensor("maskb_s", [128, 128], BF16))
        onescol_b = e_(nc.sbuf_tensor("onescol_b", [128, 1], BF16))
        epscol_f = e_(nc.sbuf_tensor("epscol_f", [128, 1], F32))
        onescol_f = e_(nc.sbuf_tensor("onescol_f", [128, 1], F32))
        onesrow_f = e_(nc.sbuf_tensor("onesrow_f", [1, 128], F32))
        alpha1 = e_(nc.sbuf_tensor("alpha1", [128, 1], F32))
        sc_sc = e_(nc.sbuf_tensor("sc_sc", [1, 1], F32))
        rscale = e_(nc.sbuf_tensor("rscale", [128, 1], F32))
        pcast = [e_(nc.sbuf_tensor(f"pcast{i}", [128, D], CDT)) for i in range(2)]
        ternb = a_m

        psA = e_(nc.psum_tensor("psA", [128, D], F32))
        psB = e_(nc.psum_tensor("psB", [128, D], F32))
        psBT = psB.ap().bitcast(BF16)

        for s in all_sems:
            s.h = e_(nc.semaphore(s.name))

        SQ_DUMP = (0, 2048)
        EL_E = [(2048, 2560), (2560, 3072)]
        EL_R = [(3072, 3584), (3584, 4096)]
        RZ = (2048, 4096)
        STMP = [(0, 512), (512, 1024)]

        def psBT_lo3d():
            return psBT[:, 0:2048].rearrange("p (t c) -> p t c", c=128)

        op("gpsimd", lambda e: e.dma_start(out=identb[:, :], in_=identb_in[:, :]),
           [(s_ini, 16)])
        op("gpsimd", lambda e: e.dma_start(out=maskb[:, :], in_=maskb_in[:, :]),
           [(s_ini, 16)])
        for m in ("q", "k", "v", "o"):
            op("gpsimd", lambda e, m=m: e.dma_start(out=biases[m][:, :],
                                                    in_=b_in[m][:, :]),
               [(s_ini, 16)])
        op("gpsimd", lambda e: e.dma_start(out=lnw_sb[:, :], in_=lnw_in[:, :]),
           [(s_ini, 16)])
        op("gpsimd", lambda e: e.dma_start(out=lnb_sb[:, :], in_=lnb_in[:, :]),
           [(s_ini, 16)])
        init_gdma = s_ini.n

        wait("scalar", s_ini, init_gdma)
        wait("vector", s_ini, init_gdma)
        wait("tensor", s_ini, init_gdma)

        op("vector", lambda e: e.memset(onescol_b[:, :], 1.0))
        op("vector", lambda e: e.memset(epscol_f[:, :], EPS_AT))
        op("vector", lambda e: e.memset(onescol_f[:, :], 1.0))
        op("vector", lambda e: e.memset(onesrow_f[:, :], 1.0))
        op("vector", lambda e: e.memset(alpha1[:, :], 1.0))
        op("vector", lambda e: e.memset(kvstate[:, :], 0.0))
        op("vector", lambda e: e.memset(kvbf[:, :], 0.0))
        op("vector", lambda e: e.memset(ksumst[:, :], 0.0))
        op("vector", lambda e: e.memset(ksumbf[:, :], 0.0), [(s_dv, 1)])
        init_dv = s_dv.n

        psA_bank_rel = [None] * 4
        pro_steps = []
        cast_rel = [None, None]
        loadslot_rel = [None, None]

        def mk_step_convert(m, r, mode):
            sl = r % 2

            def step():
                lr = loadslot_rel[sl]
                if lr is not None:
                    wait("sync", lr[0], lr[1])
                op("sync", lambda e, m=m, r=r, sl=sl: e.dma_start(
                    out=xres[sl][:, :], in_=w_in[m][r * 128:(r + 1) * 128, :]),
                   [(s_xl[sl], 16)])
                w_loaded = s_xl[sl].n
                wait("vector", s_xl[sl], w_loaded)
                if mode == "absred":
                    op("vector", lambda e, r=r, sl=sl: e.reduce_sum(
                        stats[:, r:r + 1], xres[sl][:, :], axis=AX,
                        apply_absolute_value=True), [(s_dv, 1)])
                    loadslot_rel[sl] = (s_dv, s_dv.n)
                    return
                cslot = pcast[sl]
                cr = cast_rel[sl]
                if cr is not None:
                    wait("vector", cr[0], cr[1])
                if mode == "cast":
                    op("vector", lambda e, sl=sl, cs=cslot: e.tensor_copy(
                        cs[:, :], xres[sl][:, :]), [(s_dv, 1)])
                else:
                    wait("vector", s_dv, s_dv.n)
                    op("vector", lambda e, sl=sl, cs=cslot: e.tensor_scalar(
                        cs[:, :], xres[sl][:, :], rscale[:, :], 0.5,
                        ALU.mult, ALU.is_gt))
                    op("vector", lambda e, sl=sl: e.tensor_scalar(
                        ternb[:, :], xres[sl][:, :], rscale[:, :], -0.5,
                        ALU.mult, ALU.is_lt), [(s_dv, 1)])
                    wait("vector", s_dv, s_dv.n)
                    op("vector", lambda e, cs=cslot: e.tensor_tensor(
                        cs[:, :], cs[:, :], ternb[:, :], ALU.subtract),
                       [(s_dv, 1)])
                    set_rel("a_m", (s_dv, s_dv.n))
                loadslot_rel[sl] = (s_dv, s_dv.n)
                wait("gpsimd", s_dv, s_dv.n)
                for half in range(2):
                    def bf(e, slot=cslot, m=m, r=r, half=half):
                        s2 = slot[:, :].rearrange("p (t d) -> p t d", d=128)
                        s2 = s2[:, half * 8:(half + 1) * 8, :]
                        dst = wsc[MIDX[m], half * 8:(half + 1) * 8, :, r, :]
                        return e.dma_start(out=dst.transpose([1, 0, 2]), in_=s2)
                    op("gpsimd", bf, [(s_sm[(m, sl)], 16)])
                cast_rel[sl] = (s_sm[(m, sl)], s_sm[(m, sl)].n)
            return step

        def mk_step_scale():
            def step():
                wait("vector", s_dv, s_dv.n)
                op("vector", lambda e: e.reduce_sum(
                    scr[:, 0:1], stats[:, 0:NT], axis=AX), [(s_dv, 1)])
                wait("vector", s_dv, s_dv.n)
                wait("tensor", s_dv, s_dv.n)
                r0 = psA_bank_rel[0]
                if r0 is not None:
                    wait("tensor", r0[0], r0[1])
                op("tensor", lambda e: e.matmul(
                    psA[0:1, 0:1], scr[:, 0:1], onescol_f[:, :],
                    start=True, stop=True), [(s_pe, 1)])
                wait("vector", s_pe, s_pe.n)
                vtiny(lambda e: e.tensor_scalar(
                    sc_sc[:, :], psA[0:1, 0:1], 1.0 / (D * D), None, ALU.mult))
                wait("tensor", s_dv, s_dv.n)
                op("tensor", lambda e: e.matmul(
                    psA[:, 1:2], onesrow_f[:, :], sc_sc[:, :],
                    start=True, stop=True), [(s_pe, 1)])
                wait("vector", s_pe, s_pe.n)
                vtiny(lambda e: e.tensor_copy(scr[:, 1:2], psA[:, 1:2]))
                vtiny(lambda e: e.reciprocal(rscale[:, :], scr[:, 1:2]))
                psA_bank_rel[0] = (s_dv, s_dv.n)
            return step

        for m in ("q", "k", "v", "o"):
            for r in range(NT):
                pro_steps.append(mk_step_convert(m, r, "cast"))
        for r in range(NT):
            pro_steps.append(mk_step_convert("f", r, "absred"))
        pro_steps.append(mk_step_scale())
        for r in range(NT):
            pro_steps.append(mk_step_convert("f", r, "tern"))
        PRO_READY = {"q": 16, "k": 32, "v": 48, "o": 64, "f": len(pro_steps)}
        pumped = [0]

        def pump(n):
            while n > 0 and pumped[0] < len(pro_steps):
                pro_steps[pumped[0]]()
                pumped[0] += 1
                n -= 1

        def pump_until(idx):
            pump(max(0, idx - pumped[0]))

        wtile_ctr = [0]
        pending_wfree = [0]

        def stream_whalf(m, to, half):
            k = wtile_ctr[0]
            wtile_ctr[0] += 1
            if k >= NRING:
                wait("sync", s_wfree, k - (NRING - 1))
            if to == 0 and half == 0:
                pump_until(PRO_READY[m])
                wait("sync", s_sm[(m, 0)], s_sm[(m, 0)].n)
                wait("sync", s_sm[(m, 1)], s_sm[(m, 1)].n)
            op("sync", lambda e, m=m, to=to, k=k, half=half: e.dma_start(
                out=wring[k % NRING][:, :].rearrange("p (t d) -> p t d", d=128),
                in_=wsc[MIDX[m], to, :, half * 8:(half + 1) * 8, :]),
               [(s_w[k % NRING], 16)])
            return k, (s_w[k % NRING], s_w[k % NRING].n)

        def load_xln(c):
            sl = c % 4
            wait_rel("gpsimd", f"xln{sl}")
            op("gpsimd", lambda e, c=c, sl=sl: e.dma_start(
                out=xln[sl][:, :], in_=x_in[c * 128:(c + 1) * 128, :]),
               [(s_gx[sl], 16)])
            return (s_gx[sl], s_gx[sl].n)

        def load_xres(c):
            sl = c % 2
            wait_rel("gpsimd", f"xres{sl}")
            op("gpsimd", lambda e, c=c, sl=sl: e.dma_start(
                out=xres[sl][:, :], in_=x_in[c * 128:(c + 1) * 128, :]),
               [(s_xr[sl], 16)])
            return (s_xr[sl], s_xr[sl].n)

        ac_kvbf = 0
        dv_kvbf_state = [0]
        dv_ksbf_state = [0]
        xres_handover = [False]

        pend_ln = {}

        def emit_ln_A(sc2, gate_dv):
            L_SUM, L_SQ, L_MEAN, L_M2, L_VAR, L_U, L_RU, L_W2 = 0, 4, 8, 12, 16, 20, 24, 28
            L_LNU, L_R1, L_LNW2, L_R2, L_NMR, L_SC2, L_NMR2 = 32, 36, 40, 44, 48, 52, 56
            loaded = {}
            for j in range(NCH):
                loaded[j] = load_xln(sc2 * NCH + j)
            ac_sq = 0
            for j in range(NCH):
                xs = xln[(sc2 * NCH + j) % 4]
                wait("scalar", loaded[j][0], loaded[j][1])
                if j == 0 and gate_dv:
                    wait("scalar", s_dv, gate_dv)
                if j > 0:
                    wait("scalar", s_ac, ac_sq)
                op("scalar", lambda e, xs=xs, j=j: e.activation(
                    scr[:, RZ[0]:RZ[1]], xs[:, :], AF.Square,
                    accum_out=lst[:, L_SQ + j:L_SQ + j + 1]), [(s_ac, 1)])
                ac_sq = s_ac.n
                wait("vector", loaded[j][0], loaded[j][1])
                op("vector", lambda e, xs=xs, j=j: e.reduce_sum(
                    lst[:, L_SUM + j:L_SUM + j + 1], xs[:, :], axis=AX),
                   [(s_dv, 1)])
            wait("vector", s_dv, s_dv.n)
            c4 = lambda base: lst[:, base:base + 4]
            vtiny(lambda e: e.tensor_scalar(
                c4(L_MEAN), c4(L_SUM), 1.0 / D, None, ALU.mult))
            vtiny(lambda e: e.tensor_tensor(
                c4(L_M2), c4(L_MEAN), c4(L_MEAN), ALU.mult))
            wait("vector", s_ac, ac_sq)
            vtiny(lambda e: e.scalar_tensor_tensor(
                c4(L_VAR), c4(L_SQ), 1.0 / D, c4(L_M2), ALU.mult, ALU.subtract))
            vtiny(lambda e: e.tensor_scalar(
                c4(L_U), c4(L_VAR), EPS_LN, None, ALU.add))
            vtiny(lambda e: e.reciprocal(c4(L_RU), c4(L_U)))
            vtiny(lambda e: e.tensor_tensor(
                c4(L_W2), c4(L_VAR), c4(L_RU), ALU.mult))
            vtiny(lambda e: e.tensor_scalar(
                c4(L_W2), c4(L_W2), EPS_LN, None, ALU.add))
            wait("scalar", s_dv, s_dv.n)
            atiny(lambda e: e.activation(c4(L_LNU), c4(L_U), AF.Ln))
            atiny(lambda e: e.activation(c4(L_R1), c4(L_LNU), AF.Exp, scale=-0.5))
            atiny(lambda e: e.activation(c4(L_LNW2), c4(L_W2), AF.Ln))
            atiny(lambda e: e.activation(c4(L_R2), c4(L_LNW2), AF.Exp, scale=-0.5))
            wait("vector", s_ac, s_ac.n)
            vtiny(lambda e: e.scalar_tensor_tensor(
                c4(L_NMR), c4(L_MEAN), -1.0, c4(L_R1), ALU.mult, ALU.mult))
            vtiny(lambda e: e.tensor_tensor(
                c4(L_SC2), c4(L_R1), c4(L_R2), ALU.mult))
            vtiny(lambda e: e.scalar_tensor_tensor(
                c4(L_NMR2), c4(L_MEAN), -1.0, c4(L_SC2), ALU.mult, ALU.mult))
            dv_aff = s_dv.n
            wait("scalar", s_dv, dv_aff)
            ac_applies = {}
            for j in range(2):
                c = sc2 * NCH + j
                xs = xln[c % 4]
                xntm = [kftok, vtok][j % 2]
                xnstm = xns_tm[j % 2]
                wait_rel("scalar", "kftok" if j % 2 == 0 else "vtok")
                op("scalar", lambda e, xs=xs, xm=xntm, j=j: e.activation(
                    xm[:, :], xs[:, :], AF.Identity,
                    bias=lst[:, L_NMR + j:L_NMR + j + 1],
                    scale=lst[:, L_R1 + j:L_R1 + j + 1]), [(s_ac, 1)])
                wait_rel("scalar", f"xns{j % 2}")
                op("scalar", lambda e, xs=xs, xm=xnstm, j=j: e.activation(
                    xm[:, :], xs[:, :], AF.Identity,
                    bias=lst[:, L_NMR2 + j:L_NMR2 + j + 1],
                    scale=lst[:, L_SC2 + j:L_SC2 + j + 1]), [(s_ac, 1)])
                ac_applies[j] = s_ac.n
                set_rel(f"xln{c % 4}", (s_ac, s_ac.n))
            pend_ln[sc2] = (ac_applies, s_dv.n)

        def emit_apply(sc2, j, ac_applies):
            L_R1, L_NMR, L_SC2, L_NMR2 = 36, 48, 52, 56
            c = sc2 * NCH + j
            xs = xln[c % 4]
            xntm = [kftok, vtok][j % 2]
            xnstm = xns_tm[j % 2]
            wait_rel("scalar", "kftok" if j % 2 == 0 else "vtok")
            op("scalar", lambda e, xs=xs, xm=xntm, j=j: e.activation(
                xm[:, :], xs[:, :], AF.Identity,
                bias=lst[:, L_NMR + j:L_NMR + j + 1],
                scale=lst[:, L_R1 + j:L_R1 + j + 1]), [(s_ac, 1)])
            wait_rel("scalar", f"xns{j % 2}")
            op("scalar", lambda e, xs=xs, xm=xnstm, j=j: e.activation(
                xm[:, :], xs[:, :], AF.Identity,
                bias=lst[:, L_NMR2 + j:L_NMR2 + j + 1],
                scale=lst[:, L_SC2 + j:L_SC2 + j + 1]), [(s_ac, 1)])
            ac_applies[j] = s_ac.n
            set_rel(f"xln{c % 4}", (s_ac, s_ac.n))

        def emit_ln_B(sc2):
            ac_applies, dv_aff = pend_ln.pop(sc2)
            for j in range(NCH):
                xntm = [kftok, vtok][j % 2]
                xnstm = xns_tm[j % 2]
                wait("tensor", s_ac, ac_applies[j])
                wait("tensor", s_ini, init_gdma)
                wait_rel("tensor", "psBhi")
                for t in range(NT):
                    incs = [(s_pe, 1)] if t == NT - 1 else []
                    op("tensor", lambda e, t=t, xm=xnstm: e.transpose(
                        psBT[:, 2048 + t * 128:2048 + (t + 1) * 128],
                        xm[:, t * 128:(t + 1) * 128], identb[:, :]), incs)
                pe_tr2 = s_pe.n
                wait_rel("tensor", "psBlo")
                for t in range(NT):
                    incs = [(s_pe, 1)] if t == NT - 1 else []
                    op("tensor", lambda e, t=t, xm=xntm: e.transpose(
                        psBT[:, t * 128:(t + 1) * 128],
                        xm[:, t * 128:(t + 1) * 128], identb[:, :]), incs)
                pe_tr1 = s_pe.n
                set_rel("kftok" if j % 2 == 0 else "vtok", (s_pe, pe_tr1))
                set_rel(f"xns{j % 2}", (s_pe, pe_tr2))
                if j + 2 < NCH:
                    emit_apply(sc2, j + 2, ac_applies)
                wait("vector", s_pe, pe_tr1)
                wait_rel("vector", "xnT")
                op("vector", lambda e, j=j: e.tensor_copy(
                    bass.AP(xnT, j * 128, [[NT * T, 128], [T, NT], [1, 128]]),
                    psBT_lo3d()), [(s_dv, 1)])
                dv_xnT = s_dv.n
                wait("vector", s_pe, pe_tr2)
                wait("scalar", s_pe, pe_tr2)
                wait_rel("vector", "xn2T")
                for t in range(0, NT, 2):
                    incs = [(s_dv, 1)] if t == NT - 2 else []
                    op("vector", lambda e, t=t, j=j: e.tensor_scalar(
                        xn2T[:, t * T + j * 128: t * T + (j + 1) * 128],
                        psBT[:, 2048 + t * 128:2048 + (t + 1) * 128],
                        lnw_sb[:, t:t + 1], lnb_sb[:, t:t + 1],
                        ALU.mult, ALU.add), incs)
                for t in range(1, NT, 2):
                    incs = [(s_ac, 1)] if t == NT - 1 else []
                    op("scalar", lambda e, t=t, j=j: e.activation(
                        xn2T[:, t * T + j * 128: t * T + (j + 1) * 128],
                        psBT[:, 2048 + t * 128:2048 + (t + 1) * 128],
                        AF.Identity, bias=lnb_sb[:, t:t + 1],
                        scale=lnw_sb[:, t:t + 1]), incs)
                set_rel("psBlo", (s_dv, dv_xnT))
                set_rel("psBhi", (s_dv, s_dv.n), (s_ac, s_ac.n))
                pump(2)
            return s_dv.n, s_ac.n

        for sc in range(nsc):
            if sc == 0:
                emit_ln_A(0, 0)
            dv_ln_done, ac_ln_done = emit_ln_B(sc)

            def proj_tile(m, to, evac_fn, rhs_buf, rhs_deps, ps=None):
                ps_t = psA if ps is None else ps
                kA, wnA = stream_whalf(m, to, 0)
                kB, wnB = stream_whalf(m, to, 1)
                bk = to % 4
                if ps is None:
                    r0 = psA_bank_rel[bk]
                    if r0 is not None:
                        wait("tensor", r0[0], r0[1])
                if to == 0:
                    for sem, cnt in rhs_deps:
                        wait("tensor", sem, cnt)
                wait("tensor", wnA[0], wnA[1])
                for ti in range(NT):
                    k = kA if ti < 8 else kB
                    incs = []
                    if ti == 0 and pending_wfree[0]:
                        incs = [(s_wfree, pending_wfree[0])]
                        pending_wfree[0] = 0
                    if ti == 8:
                        incs = [(s_wfree, 1)]
                        wait("tensor", wnB[0], wnB[1])
                    if ti == NT - 1:
                        incs = [(s_pe, 1)]
                    op("tensor", lambda e, k=k, ti=ti, bk=bk, rb=rhs_buf, pt=ps_t:
                       e.matmul(
                           pt[:, bk * 512:(bk + 1) * 512],
                           wring[k % NRING][:, (ti % 8) * 128:(ti % 8 + 1) * 128],
                           rb[:, ti * T:(ti + 1) * T],
                           start=(ti == 0), stop=(ti == NT - 1)), incs)
                pending_wfree[0] += 1
                evac_fn(to, bk, s_pe.n)
                pump(2)

            def proj_phase(m, evac_fn, rhs_buf, rhs_deps):
                for to in range(NT):
                    proj_tile(m, to, evac_fn, rhs_buf, rhs_deps)

            def mk_evac_qk(dst, bias_key):
                def evac(to, bk, pe_n):
                    esl = EL_E[to % 2]
                    rsl = EL_R[to % 2]
                    wait("scalar", s_pe, pe_n)
                    wait_rel("scalar", f"el{to % 2}")
                    op("scalar", lambda e, bk=bk, to=to, esl=esl: e.activation(
                        scr[:, esl[0]:esl[1]], psA[:, bk * 512:(bk + 1) * 512],
                        AF.Exp, bias=biases[bias_key][:, to:to + 1]), [(s_ac, 1)])
                    op("scalar", lambda e, bk=bk, to=to, rsl=rsl: e.activation(
                        scr[:, rsl[0]:rsl[1]], psA[:, bk * 512:(bk + 1) * 512],
                        AF.Relu, bias=biases[bias_key][:, to:to + 1]), [(s_ac, 1)])
                    psA_bank_rel[bk] = (s_ac, s_ac.n)
                    wait("vector", s_ac, s_ac.n)
                    op("vector", lambda e, to=to, esl=esl, rsl=rsl:
                       e.scalar_tensor_tensor(
                           dst[:, to * T:(to + 1) * T], scr[:, esl[0]:esl[1]], 1.0,
                           scr[:, rsl[0]:rsl[1]], ALU.min, ALU.add), [(s_dv, 1)])
                    set_rel(f"el{to % 2}", (s_dv, s_dv.n))
                return evac

            proj_phase("q", mk_evac_qk(qat, "q"), xnT, [(s_dv, dv_ln_done)])
            q_done_dv = s_dv.n
            proj_phase("k", mk_evac_qk(kfT, "k"), xnT, [(s_dv, dv_ln_done)])
            k_done_dv = s_dv.n
            wait("vector", s_dv, s_dv.n)
            for j in range(NCH):
                op("vector", lambda e, j=j: e.reduce_sum(
                    bass.AP(kscur, j * H, [[4 * H, 128], [1, H], [0, 1]]),
                    bass.AP(kfT, j * 128, [[NT * T, 128], [T, H], [1, 128]]),
                    axis=AX), [(s_dv, 1)])
            dv_ksum4 = s_dv.n

            def evac_v(to, bk, pe_n):
                wait("scalar", s_pe, pe_n)
                op("scalar", lambda e, to=to, bk=bk: e.activation(
                    vg[:, to * T:(to + 1) * T], psA[:, bk * 512:(bk + 1) * 512],
                    AF.Identity, bias=biases["v"][:, to:to + 1],
                    scale=1.0), [(s_ac, 1)])
                psA_bank_rel[bk] = (s_ac, s_ac.n)

            proj_phase("v", evac_v, xnT, [(s_dv, dv_ln_done)])
            v_done_ac = s_ac.n

            def evac_f(to, bk, pe_n):
                wait("scalar", s_pe, pe_n)
                op("scalar", lambda e, to=to, bk=bk: e.activation(
                    xnT[:, to * T:(to + 1) * T], psB[:, bk * 512:(bk + 1) * 512],
                    AF.Silu), [(s_ac, 1)])
                add_rel("psBlo" if bk < 2 else "psBhi", (s_ac, s_ac.n))

            for j in range(NCH):
                wait_rel("tensor", "psBlo")
                wait("tensor", s_dv, k_done_dv)
                for h in range(H):
                    incs = [(s_pe, 1)] if h == H - 1 else []
                    op("tensor", lambda e, h=h, j=j: e.transpose(
                        psBT[:, h * 128:(h + 1) * 128],
                        kfT[:, h * T + j * 128: h * T + (j + 1) * 128],
                        identb[:, :]), incs)
                pe_kft = s_pe.n
                wait_rel("tensor", "psBhi")
                wait("tensor", s_ac, v_done_ac)
                for h in range(H):
                    incs = [(s_pe, 1)] if h == H - 1 else []
                    op("tensor", lambda e, h=h, j=j: e.transpose(
                        psBT[:, 2048 + h * 128:2048 + (h + 1) * 128],
                        vg[:, h * T + j * 128: h * T + (j + 1) * 128],
                        identb[:, :]), incs)
                pe_vt = s_pe.n
                wait("scalar", s_pe, pe_kft)
                wait_rel("scalar", "kftok")
                op("scalar", lambda e: e.copy(
                    kftok[:, :], psBT[:, 0:2048]), [(s_ac, 1)])
                ac_kftok = s_ac.n
                wait("scalar", s_pe, pe_vt)
                wait_rel("scalar", "vtok")
                op("scalar", lambda e: e.copy(
                    vtok[:, :], psBT[:, 2048:4096]), [(s_ac, 1)])
                ac_vtok = s_ac.n
                wait("tensor", s_dv, q_done_dv)
                for bk in range(4):
                    r0 = psA_bank_rel[bk]
                    if r0 is not None:
                        wait("tensor", r0[0], r0[1])
                for h in range(H):
                    incs = [(s_pe, 1)] if h == H - 1 else []
                    op("tensor", lambda e, h=h, j=j: e.matmul(
                        psA[:, h * 128:(h + 1) * 128],
                        kfT[:, h * T + j * 128: h * T + (j + 1) * 128],
                        qat[:, h * T + j * 128: h * T + (j + 1) * 128],
                        start=True, stop=True), incs)
                pe_a = s_pe.n
                wait("vector", s_pe, pe_a)
                wait_rel("vector", "a_m")
                op("vector", lambda e: e.tensor_tensor(
                    a_m[:, 0:1024].rearrange("p (t c) -> p t c", c=128),
                    psA[:, 0:1024].rearrange("p (t c) -> p t c", c=128),
                    maskb[:, :].unsqueeze(1).broadcast_to([128, H // 2, 128]),
                    ALU.mult), [(s_dv, 1)])
                dv_am_a = s_dv.n
                op("vector", lambda e: e.tensor_tensor(
                    a_m[:, 1024:2048].rearrange("p (t c) -> p t c", c=128),
                    psA[:, 1024:2048].rearrange("p (t c) -> p t c", c=128),
                    maskb[:, :].unsqueeze(1).broadcast_to([128, H // 2, 128]),
                    ALU.mult), [(s_dv, 1)])
                dv_am_b = s_dv.n
                wait("tensor", s_dv, dv_ksbf_state[0])
                for h in range(H):
                    if h == 0:
                        wait("tensor", s_dv, dv_am_a)
                    if h == H // 2:
                        wait("tensor", s_dv, dv_am_b)
                    op("tensor", lambda e, h=h, j=j: e.matmul(
                        psA[:, h * 128:(h + 1) * 128],
                        bass.AP(ksumbf, h, [[H, 128], [0, 128]]),
                        qat[:, h * T + j * 128: h * T + (j + 1) * 128],
                        start=True, stop=False))
                    incs = [(s_pe, 1)] if h == H - 1 else []
                    op("tensor", lambda e, h=h: e.matmul(
                        psA[:, h * 128:(h + 1) * 128],
                        bass.AP(onescol_b, 0, [[1, 128], [0, 128]]),
                        a_m[:, h * 128:(h + 1) * 128],
                        start=False, stop=True), incs)
                pe_z = s_pe.n
                wait("tensor", s_ac, ac_vtok)
                wait("tensor", s_ac, ac_kftok)
                wait("tensor", s_dv, dv_kvbf_state[0])
                for h in range(H):
                    op("tensor", lambda e, h=h, j=j: e.matmul(
                        psB[:, h * 128:(h + 1) * 128],
                        kvbf[:, h * 128:(h + 1) * 128],
                        qat[:, h * T + j * 128: h * T + (j + 1) * 128],
                        start=True, stop=False))
                    incs = [(s_pe, 1)] if h == H - 1 else []
                    op("tensor", lambda e, h=h: e.matmul(
                        psB[:, h * 128:(h + 1) * 128],
                        vtok[:, h * 128:(h + 1) * 128],
                        a_m[:, h * 128:(h + 1) * 128],
                        start=False, stop=True), incs)
                pe_io = s_pe.n
                wait("scalar", s_pe, pe_z)
                op("scalar", lambda e: e.activation(
                    scr[:, SQ_DUMP[0]:SQ_DUMP[1]], psA[:, :], AF.Ln,
                    bias=epscol_f[:, :]), [(s_ac, 1)])
                ac_lnz = s_ac.n
                wait("scalar", s_ac, ac_lnz)
                op("scalar", lambda e: e.activation(
                    scr[:, RZ[0]:RZ[1]], scr[:, SQ_DUMP[0]:SQ_DUMP[1]],
                    AF.Exp, scale=-1.0), [(s_ac, 1)])
                ac_rz = s_ac.n
                wait("tensor", s_ac, ac_lnz)
                for h in range(H):
                    incs = [(s_pe, 1)] if h == H - 1 else []
                    op("tensor", lambda e, h=h: e.matmul(
                        psA[:, h * 128:(h + 1) * 128],
                        kftok[:, h * 128:(h + 1) * 128],
                        vtok[:, h * 128:(h + 1) * 128],
                        start=True, stop=True), incs)
                pe_kv = s_pe.n
                wait("vector", s_pe, pe_io)
                wait("vector", s_ac, ac_rz)
                op("vector", lambda e, j=j: e.tensor_tensor(
                    bass.AP(qat, j * 128, [[NT * T, 128], [T, H // 2], [1, 128]]),
                    psB[:, 0:1024].rearrange("p (t c) -> p t c", c=128),
                    scr[:, RZ[0]:RZ[0] + 1024].rearrange("p (t c) -> p t c", c=128),
                    ALU.mult), [(s_dv, 1)])
                dv_out_lo = s_dv.n
                op("vector", lambda e, j=j: e.tensor_tensor(
                    bass.AP(qat, j * 128 + (H // 2) * T,
                            [[NT * T, 128], [T, H // 2], [1, 128]]),
                    psB[:, 1024:2048].rearrange("p (t c) -> p t c", c=128),
                    scr[:, RZ[0] + 1024:RZ[1]].rearrange("p (t c) -> p t c", c=128),
                    ALU.mult), [(s_dv, 1)])
                dv_out_hi = s_dv.n
                set_rel("psBlo", (s_dv, dv_out_lo))
                set_rel("psBhi", (s_dv, dv_out_hi))
                wait("vector", s_pe, pe_kv)
                op("vector", lambda e: e.tensor_tensor(
                    kvbf[:, :], kvstate[:, :], psA[:, :], ALU.add), [(s_dv, 1)])
                dv_kvbf = s_dv.n
                wait("vector", s_dv, dv_ksum4)
                op("vector", lambda e, j=j: e.tensor_tensor(
                    ksumbf[:, :], ksumst[:, :],
                    kscur[:, j * H:(j + 1) * H], ALU.add), [(s_dv, 1)])
                dv_ksbf = s_dv.n
                op("vector", lambda e: e.tensor_tensor(
                    kvstate[:, :], kvstate[:, :], psA[:, :], ALU.add), [(s_dv, 1)])
                dv_kvupd = s_dv.n
                for bk in range(4):
                    psA_bank_rel[bk] = (s_dv, dv_kvupd)
                wait("vector", s_dv, dv_ksbf)
                vtiny(lambda e, j=j: e.tensor_tensor(
                    ksumst[:, :], ksumst[:, :],
                    kscur[:, j * H:(j + 1) * H], ALU.add))
                dv_kvbf_state[0] = dv_kvbf
                dv_ksbf_state[0] = dv_ksbf
                set_rel("kftok", (s_pe, pe_kv))
                set_rel("vtok", (s_pe, pe_kv))
                set_rel("a_m", (s_pe, pe_io))
                for to in range(4 * j, 4 * j + 4):
                    if to % 4 == 0:
                        wait_rel("tensor", "psBlo")
                        set_rel("psBlo")
                    if to % 4 == 2:
                        wait_rel("tensor", "psBhi")
                        set_rel("psBhi")
                    proj_tile("f", to, evac_f, xn2T,
                              [(s_ac, ac_ln_done)], ps=psB)
            f_done_ac = s_ac.n
            attn_dv_done = s_dv.n
            set_rel("el0", (s_dv, attn_dv_done))
            set_rel("el1", (s_dv, attn_dv_done))

            if sc + 1 < nsc:
                emit_ln_A(sc + 1, attn_dv_done)

            if not xres_handover[0]:
                pump_until(len(pro_steps))
                for sl2 in range(2):
                    if loadslot_rel[sl2] is not None:
                        set_rel(f"xres{sl2}", loadslot_rel[sl2])
                xres_handover[0] = True
            xres_loaded = {}
            xres_loaded[0] = load_xres(sc * NCH + 0)
            xres_loaded[1] = load_xres(sc * NCH + 1)

            dv_sg = {}

            def evac_o(to, bk, pe_n):
                ssl = STMP[to % 2]
                wait("scalar", s_pe, pe_n)
                wait_rel("scalar", f"st{to % 2}")
                op("scalar", lambda e, to=to, bk=bk, ssl=ssl: e.activation(
                    scr[:, ssl[0]:ssl[1]], psA[:, bk * 512:(bk + 1) * 512],
                    AF.Identity, bias=biases["o"][:, to:to + 1], scale=1.0,
                    ), [(s_ac, 1)])
                psA_bank_rel[bk] = (s_ac, s_ac.n)
                wait("vector", s_ac, s_ac.n)
                if to == 0:
                    wait("vector", s_ac, f_done_ac)
                op("vector", lambda e, to=to, ssl=ssl: e.tensor_tensor(
                    vg[:, to * T:(to + 1) * T], scr[:, ssl[0]:ssl[1]],
                    xnT[:, to * T:(to + 1) * T], ALU.mult), [(s_dv, 1)])
                dv_sg[to] = s_dv.n
                set_rel(f"st{to % 2}", (s_dv, s_dv.n))

            proj_phase("o", evac_o, qat, [(s_dv, attn_dv_done)])
            sg_dv_done = s_dv.n

            for j in range(NCH):
                c = sc * NCH + j
                wait_rel("tensor", "psBlo")
                for t in range(NT):
                    wait("tensor", s_dv, dv_sg[t])
                    incs = [(s_pe, 1)] if t == NT - 1 else []
                    op("tensor", lambda e, t=t, j=j: e.transpose(
                        psBT[:, t * 128:(t + 1) * 128],
                        vg[:, t * T + j * 128: t * T + (j + 1) * 128],
                        identb[:, :]), incs)
                pe_sg = s_pe.n
                sl = c % 2
                wait("vector", s_pe, pe_sg)
                wait("vector", xres_loaded[j][0], xres_loaded[j][1])
                op("vector", lambda e, sl=sl: e.tensor_tensor(
                    xres[sl][:, :].rearrange("p (t d) -> p t d", d=128),
                    psBT_lo3d(),
                    xres[sl][:, :].rearrange("p (t d) -> p t d", d=128),
                    ALU.add), [(s_dv, 1)])
                dv_out = s_dv.n
                set_rel("psBlo", (s_dv, dv_out))
                wait("gpsimd", s_dv, dv_out)
                op("gpsimd", lambda e, c=c, sl=sl: e.dma_start(
                    out=y_out[c * 128:(c + 1) * 128, :], in_=xres[sl][:, :]),
                   [(s_xr[sl], 16)])
                set_rel(f"xres{sl}", (s_xr[sl], s_xr[sl].n))
                if j + 2 < NCH:
                    xres_loaded[j + 2] = load_xres(sc * NCH + j + 2)
            set_rel("xnT", (s_dv, sg_dv_done))
            set_rel("vg_next", (s_pe, s_pe.n))

        wait("gpsimd", s_xr[0], s_xr[0].n)
        wait("gpsimd", s_xr[1], s_xr[1].n)

        if debug:
            dbg_specs = [
                ("xnT", xnT, CDT), ("xn2T", xn2T, CDT), ("qat", qat, CDT),
                ("kfT", kfT, CDT), ("vg", vg, CDT), ("kvstate", kvstate, F32),
                ("stats", stats, F32), ("rscale", rscale, F32),
                ("kftok", kftok, CDT), ("vtok", vtok, CDT), ("a_m", a_m, CDT),
                ("ksumst", ksumst, F32), ("scr", scr, F32),
                ("kvbf", kvbf, CDT),
            ]
            wait("gpsimd", s_dv, s_dv.n)
            wait("gpsimd", s_ac, s_ac.n)
            wait("gpsimd", s_pe, s_pe.n)
            for name, buf, dt in dbg_specs:
                dbg_out = nc.dram_tensor("dbg_" + name, list(buf.shape), dt,
                                         kind="ExternalOutput")
                op("gpsimd", lambda e, o=dbg_out, b=buf: e.dma_start(
                    out=o[:, :], in_=b[:, :]), [(s_ini, 16)])
            wait("gpsimd", s_ini, s_ini.n)

        block = e_(nc.Block())

        @block.sync
        def _(sync):
            for fn in prog["sync"]:
                fn(sync)

        @block.gpsimd
        def _(gpsimd):
            for fn in prog["gpsimd"]:
                fn(gpsimd)

        @block.scalar
        def _(scalar):
            for fn in prog["scalar"]:
                fn(scalar)

        @block.vector
        def _(vector):
            for fn in prog["vector"]:
                fn(vector)

        @block.tensor
        def _(tensor):
            for fn in prog["tensor"]:
                fn(tensor)

    return nc


_NC_CACHE = {}


def _get_nc(nsc):
    if nsc not in _NC_CACHE:
        _NC_CACHE[nsc] = build_nc(nsc)
    return _NC_CACHE[nsc]


def kernel(L0_attn_x, L0_attn_Wq, L0_attn_bq, L0_attn_Wk, L0_attn_bk,
           L0_attn_Wv, L0_attn_bv, L0_attn_Wo, L0_attn_bo,
           L0_ffn_W, L0_ffn_ln_w, L0_ffn_ln_b):
    nc = _get_nc(S // T)
    x = np.ascontiguousarray(np.asarray(L0_attn_x, dtype=np.float32))

    def colmajor16(v):
        return np.ascontiguousarray(
            np.asarray(v, dtype=np.float32).reshape(NT, 128).T)

    common = {
        "w_q": np.ascontiguousarray(np.asarray(L0_attn_Wq, dtype=np.float32)),
        "w_k": np.ascontiguousarray(np.asarray(L0_attn_Wk, dtype=np.float32)),
        "w_v": np.ascontiguousarray(np.asarray(L0_attn_Wv, dtype=np.float32)),
        "w_o": np.ascontiguousarray(np.asarray(L0_attn_Wo, dtype=np.float32)),
        "w_f": np.ascontiguousarray(np.asarray(L0_ffn_W, dtype=np.float32)),
        "b_q": colmajor16(L0_attn_bq),
        "b_k": colmajor16(L0_attn_bk),
        "b_v": colmajor16(L0_attn_bv),
        "b_o": colmajor16(L0_attn_bo),
        "lnw": colmajor16(L0_ffn_ln_w),
        "lnb": colmajor16(L0_ffn_ln_b),
        "identb": np.eye(128, dtype=ml_dtypes.bfloat16),
        "maskb": np.triu(np.ones((128, 128), dtype=ml_dtypes.bfloat16)),
    }
    in_maps = []
    for b in range(N_CORES):
        m = dict(common)
        m["x"] = np.ascontiguousarray(x[b])
        in_maps.append(m)
    res = run_bass_kernel_spmd(nc, in_maps, core_ids=list(range(N_CORES)))
    out = np.stack([res.results[b]["y"] for b in range(N_CORES)], axis=0)
    return out
